# version 14
# baseline (speedup 1.0000x reference)
import contextlib
import numpy as np
import ml_dtypes
import concourse.bass as bass
import concourse.mybir as mybir
from concourse.bass_utils import run_bass_kernel_spmd

F32 = mybir.dt.float32
BF16 = mybir.dt.bfloat16
AF = mybir.ActivationFunctionType
ALU = mybir.AluOpType
AX = mybir.AxisListType

B_PER_CORE = 2
T = 2048
C = 1024
NT = T // 128
NC = C // 128
EPS_LN = 1e-5
EPS_B = 1e-6
H = 16
DH = 64
F_FFN = 4096
STRIDE = 2049
DBG_STOP = None
DBG_SCAN = None
USE_MM_TRANSPOSE = True
SKIP_EVAC = False


class Em:
    ENGS = ("T", "V", "S", "G", "Y")

    def __init__(self, nc, st):
        self.nc = nc
        self.ops = {e: [] for e in self.ENGS}
        self.sems = {}
        self.cnt = {}
        self.st = st
        for name in ("pT", "pV", "pS", "pG", "dY", "dG", "dS", "dGA", "dGB", "dGW"):
            self.sems[name] = st.enter_context(nc.semaphore(name))
            self.cnt[name] = 0
        self.last = {e: 0 for e in self.ENGS}

    def _wait_list(self, eng_obj, waits):
        best = {}
        for s, v in waits:
            if not v:
                continue
            assert v <= self.cnt[s], f"wait on future {s}={v} > {self.cnt[s]}"
            best[s] = max(best.get(s, 0), v)
        for s, v in best.items():
            eng_obj.wait_ge(self.sems[s], v)

    def op(self, eng, fn, waits=(), chain=None):
        if chain is None:
            chain = eng in ("V", "S", "G")
        prog = "p" + eng
        waits = list(waits)
        if chain and self.last[eng]:
            waits.append((prog, self.last[eng]))
        self.cnt[prog] += 1
        tok = self.cnt[prog]
        self.last[eng] = tok
        sem = self.sems[prog]
        me = self

        def run(eng_obj, fn=fn, waits=tuple(waits), sem=sem):
            me._wait_list(eng_obj, waits)
            inst = fn(eng_obj)
            assert inst is not None, "compute op must return its instruction"
            inst.then_inc(sem, 1)
        self.ops[eng].append(run)
        return tok

    def waits(self, eng, waits):
        me = self
        waits = tuple(waits)

        def run(eng_obj, waits=waits):
            me._wait_list(eng_obj, waits)
        self.ops[eng].append(run)

    def dma(self, eng, dst, src, ch=None):
        if ch is not None:
            sem = "dG" + ch
        else:
            sem = "dG" if eng == "G" else ("dS" if eng == "S" else "dY")
        self.cnt[sem] += 16
        tok = self.cnt[sem]
        s = self.sems[sem]
        me = self

        def run(eng_obj, dst=dst, src=src, s=s):
            eng_obj.dma_start(dst, src).then_inc(s, 16)
        self.ops[eng].append(run)
        return (sem, tok)

    def dma_waits(self, eng, waits):
        self.waits(eng, waits)

    def snap(self):
        return [(s, self.cnt[s]) for s in ("pT", "pV", "pS", "pG", "dY", "dG",
                                           "dS", "dGA", "dGB", "dGW")]

    def barrier(self, exclude=()):
        sn = [w for w in self.snap() if w[0] not in exclude]
        for e in ("T", "V", "S", "G", "Y"):
            self.waits(e, [w for w in sn if w[0] != "p" + e])

    def build(self, blk):
        for eng, attr in (("Y", "sync"), ("G", "gpsimd"), ("T", "tensor"),
                          ("S", "scalar"), ("V", "vector")):
            ops = self.ops[eng]

            def make(ops=ops):
                def body(eng_obj):
                    for f in ops:
                        f(eng_obj)
                return body
            getattr(blk, attr)(make())


def _bf(x):
    return np.ascontiguousarray(x, dtype=ml_dtypes.bfloat16)


def _consts_host():
    ident = np.eye(128, dtype=ml_dtypes.bfloat16)
    tri = np.triu(np.ones((128, 128), np.float32))
    triu4 = _bf(np.tile(tri, (1, 4)))
    triu_f = _bf(tri)
    jj = np.arange(128)[:, None]
    uu = np.arange(896)[None, :]
    dband = _bf((jj <= uu - 384).astype(np.float32))
    ones_row = np.ones((1, 128), np.float32)
    ones_row_bf = np.ones((1, 128), dtype=ml_dtypes.bfloat16)
    ones_col_bf = np.ones((128, 1), dtype=ml_dtypes.bfloat16)
    lpos = (np.arange(128)[:, None] + 1 + 128 * np.arange(NT)[None, :]).astype(np.float32)
    rlpos = (1.0 / lpos).astype(np.float32)
    half64 = np.full((64, 1), 0.5, np.float32)
    half64_hi = np.full((128, 1), 0.5, np.float32)
    return dict(c_ident=ident, c_triu4=triu4, c_triu=triu_f, c_dband=dband,
                c_ones_row=ones_row, c_ones_row_bf=ones_row_bf,
                c_ones_col=ones_col_bf, c_lpos=lpos, c_rlpos=rlpos,
                c_half64=half64, c_half64_hi=half64_hi)


CONST_SPECS = dict(
    c_ident=([128, 128], BF16), c_triu4=([128, 512], BF16), c_triu=([128, 128], BF16),
    c_dband=([128, 896], BF16), c_ones_row=([1, 128], F32),
    c_ones_row_bf=([1, 128], BF16), c_ones_col=([128, 1], BF16),
    c_lpos=([128, NT], F32), c_rlpos=([128, NT], F32),
    c_half64=([64, 1], F32), c_half64_hi=([128, 1], F32),
)

PARAM_COLS = ["mur_col", "muk_col", "bq_col", "bk_col", "bv_col", "bo_col", "b2_col"]


def build_program(subs=(1, 2, 3, 4), ffn_stage=5, nt_lim=NT, skip_gbc=False, skip_params=False):
    nc = bass.Bass("TRN2", target_bir_lowering=False)
    st = contextlib.ExitStack()
    D = {}

    def din(name, shape, dt=F32):
        D[name] = nc.dram_tensor(name, shape, dt, kind="ExternalInput").ap()
        return D[name]

    x_in = din("x", [B_PER_CORE, T, C])
    out = nc.dram_tensor("out", [B_PER_CORE, T, C], F32, kind="ExternalOutput").ap()

    din("wipT", [C, 3 * C]); din("woaT", [C, C])
    din("wrT", [C, C]); din("wkT", [C, C]); din("wvT", [C, C])
    din("wq", [C, C]); din("wk", [C, C]); din("wv", [C, C]); din("wo", [C, C])
    din("w1", [C, F_FFN]); din("w2", [F_FFN, C])
    for nm in PARAM_COLS:
        din(nm, [128, 8])
    din("b1_col", [128, 32])
    din("bv_row", [1, C]); din("bo_row", [1, C])
    din("g1_row", [1, C]); din("b1ln_row", [1, C])
    din("g2_row", [1, C]); din("b2ln_row", [1, C])
    for k, (shape, dt) in CONST_SPECS.items():
        din(k, shape, dt)

    em = Em(nc, st)

    def sb(name, p, f, dt=BF16):
        return st.enter_context(nc.sbuf_tensor(name, [p, f], dt))

    def psum(name, p, f, dt=F32):
        return st.enter_context(nc.psum_tensor(name, [p, f], dt))

    x_sb = sb("x_sb", 128, NT * 1024, F32)
    A = sb("bufA", 128, NC * STRIDE)
    Bb = sb("bufB", 128, 16384)
    Cc = sb("bufC", 128, 16384)
    Dd = sb("bufD", 128, 16384)
    cs = {k: sb(k + "_s", sh[0], sh[1], dt) for k, (sh, dt) in CONST_SPECS.items()}
    f32s = sb("f32s", 128, 800, F32)
    pcols = {nm: sb(nm + "_s", 128, 8, F32) for nm in PARAM_COLS}
    b1_c = sb("b1_col_s", 128, 32, F32)
    g_bc = sb("g_bc", 128, 1024); b_bc = sb("b_bc", 128, 1024)
    sc32 = sb("sc32", 128, 1024, F32)

    eps_ln_c = f32s[:, 32:33]
    eps_b_c = f32s[:, 33:34]
    em.op("V", lambda v: v.memset(f32s[:, 32:33], EPS_LN))
    em.op("V", lambda v: v.memset(f32s[:, 33:34], EPS_B))

    pk = [psum(f"pk{i}", 128, 512) for i in range(6)]
    pk6 = psum("pk6", 128, 512)
    pk7 = psum("pk7", 128, 512)
    wpf = {}
    xload_toks = {}
    pc0 = pk6
    ptr_guard = {}
    negmargin_c = f32s[:, 34 + 64:34 + 65] if False else f32s[:, 798:799]
    em.op("V", lambda v: v.memset(f32s[:, 798:799], -14.0))

    ln_slot_guard = {}

    def emit_ln_tile(xt_ap, u_out_ap, stat_off, extra_wait=(), apply_wait=()):
        bns = f32s[:, stat_off + 2:stat_off + 14]
        agg = f32s[:, stat_off + 14:stat_off + 16]
        mcol = f32s[:, stat_off:stat_off + 1]
        rcol = f32s[:, stat_off + 1:stat_off + 2]
        g = ln_slot_guard.get(stat_off)
        em.op("V", lambda v: v.bn_stats(bns[:, 0:6], xt_ap[:, 0:512]),
              waits=list(extra_wait) + ([("pS", g)] if g else []))
        em.op("V", lambda v: v.bn_stats(bns[:, 6:12], xt_ap[:, 512:1024]))
        t3 = em.op("V", lambda v: v.bn_aggr(agg, bns))
        t4 = em.op("S", lambda s: s.activation(rcol, agg[:, 1:2], AF.Sqrt,
                                               bias=eps_ln_c, scale=1.0),
                   waits=[("pV", t3)])
        t5 = em.op("V", lambda v: v.reciprocal(rcol, rcol), waits=[("pS", t4)])
        t6 = em.op("V", lambda v: v.tensor_scalar(mcol, agg[:, 0:1], rcol, -1.0,
                                                  ALU.mult, ALU.mult))
        t7 = em.op("S", lambda s: s.activation(u_out_ap, xt_ap, AF.Identity,
                                               bias=mcol, scale=rcol),
                   waits=[("pV", t6)] + list(apply_wait))
        ln_slot_guard[stat_off] = t7
        return t7, mcol, rcol

    def emit_transpose_1024(src_fn, dst_fn, bank_name, wait_src, gdict=None):
        bpair = (4, 5) if bank_name == "ptr" else (2, 3)
        last_T = 0
        for r in range(2):
            bank = pk[bpair[r]]
            toks = []
            for j in range(4):
                jj = r * 4 + j
                w = list(wait_src if jj == 0 else [])
                if gdict is None:
                    g = ptr_guard.get((bank_name, r), 0)
                    if j == 0 and g:
                        w.append(("pV", g))
                else:
                    g = gdict.get(bpair[r])
                    if j == 0 and g:
                        w.append(g)
                toks.append(em.op("T", lambda t, jj=jj, j=j, bank=bank: t.matmul(
                    bank[:, 128 * j:128 * (j + 1)], src_fn(jj), cs["c_ident"][:],
                    start=True, stop=True),
                    waits=w, chain=False))
            last_T = toks[-1]
            for j in range(4):
                jj = r * 4 + j
                ev = em.op("V", lambda v, jj=jj, j=j, bank=bank: v.tensor_copy(
                    dst_fn(jj), bank[:, 128 * j:128 * (j + 1)]),
                    waits=[("pT", toks[-1])] if j == 0 else [])
            if gdict is None:
                ptr_guard[(bank_name, r)] = ev
            else:
                gdict[bpair[r]] = ("pV", ev)
        return ev, last_T


    ln_slot_guard = {}

    def emit_ln_tile(xt_ap, u_out_ap, stat_off, extra_wait=(), apply_wait=()):
        bns = f32s[:, stat_off + 2:stat_off + 14]
        agg = f32s[:, stat_off + 14:stat_off + 16]
        mcol = f32s[:, stat_off:stat_off + 1]
        rcol = f32s[:, stat_off + 1:stat_off + 2]
        g = ln_slot_guard.get(stat_off)
        em.op("V", lambda v: v.bn_stats(bns[:, 0:6], xt_ap[:, 0:512]),
              waits=list(extra_wait) + ([("pS", g)] if g else []))
        em.op("V", lambda v: v.bn_stats(bns[:, 6:12], xt_ap[:, 512:1024]))
        t3 = em.op("V", lambda v: v.bn_aggr(agg, bns))
        t4 = em.op("S", lambda s: s.activation(rcol, agg[:, 1:2], AF.Sqrt,
                                               bias=eps_ln_c, scale=1.0),
                   waits=[("pV", t3)])
        t5 = em.op("V", lambda v: v.reciprocal(rcol, rcol), waits=[("pS", t4)])
        t6 = em.op("V", lambda v: v.tensor_scalar(mcol, agg[:, 0:1], rcol, -1.0,
                                                  ALU.mult, ALU.mult))
        t7 = em.op("S", lambda s: s.activation(u_out_ap, xt_ap, AF.Identity,
                                               bias=mcol, scale=rcol),
                   waits=[("pV", t6)] + list(apply_wait))
        ln_slot_guard[stat_off] = t7
        return t7, mcol, rcol


    def emit_ln_pass_batched(tiles, dst_fn, scratch_off, wait_extra=(), gdict=None,
                             tile_waits=None):
        ag_last = None
        first = True
        for ii in tiles:
            so = 100 + 16 * ii
            g = ln_slot_guard.get(so)
            w = (list(wait_extra) if first else []) + ([("pS", g)] if g else [])
            if tile_waits is not None:
                w = w + list(tile_waits(ii))
            first = False
            em.op("V", lambda v, ii=ii, so=so: v.bn_stats(
                f32s[:, so + 2:so + 8], x_sb[:, ii * 1024:ii * 1024 + 512]), waits=w)
            em.op("V", lambda v, ii=ii, so=so: v.bn_stats(
                f32s[:, so + 8:so + 14], x_sb[:, ii * 1024 + 512:(ii + 1) * 1024]))
            ag_last = em.op("V", lambda v, so=so: v.bn_aggr(
                f32s[:, so + 14:so + 16], f32s[:, so + 2:so + 14]))
        n = len(tiles)
        i0 = tiles[0]
        slots = f32s[:, 100 + 16 * i0:100 + 16 * (i0 + n)].rearrange(
            "p (t c) -> p t c", c=16)
        gv = em.op("V", lambda v, slots=slots, n=n: v.tensor_scalar(
            f32s[:, 360:360 + n], slots[:, :, 15], EPS_LN, None, ALU.add))
        sq = em.op("S", lambda s, n=n: s.activation(f32s[:, 360:360 + n],
                                                    f32s[:, 360:360 + n], AF.Sqrt),
                   waits=[("pV", gv)])
        em.op("V", lambda v, n=n: v.reciprocal(f32s[:, 360:360 + n],
                                               f32s[:, 360:360 + n]),
              waits=[("pS", sq)])
        em.op("V", lambda v, slots=slots: v.tensor_copy(
            slots[:, :, 1], f32s[:, 360:360 + n]))
        em.op("V", lambda v, slots=slots: v.tensor_mul(
            f32s[:, 376:376 + n], slots[:, :, 14], f32s[:, 360:360 + n]))
        stats_done = em.op("V", lambda v, slots=slots: v.tensor_scalar(
            slots[:, :, 0], f32s[:, 376:376 + n], -1.0, None, ALU.mult))
        lt_v = lt_T = 0
        for i, ii in enumerate(tiles):
            so = 100 + 16 * ii
            ew = [("pT", lt_T)] if lt_T else []
            t7 = em.op("S", lambda s, ii=ii, so=so: s.activation(
                Cc[:, scratch_off:scratch_off + 1024],
                x_sb[:, ii * 1024:(ii + 1) * 1024], AF.Identity,
                bias=f32s[:, so:so + 1], scale=f32s[:, so + 1:so + 2]),
                waits=[("pV", stats_done)] + ew)
            ln_slot_guard[so] = t7
            lt_v, lt_T = emit_transpose_1024(
                lambda j: Cc[:, scratch_off + 128 * j:scratch_off + 128 * (j + 1)],
                lambda j, i=i, ii=ii: dst_fn(ii, j),
                "ptr", wait_src=[("pS", t7)], gdict=gdict)
        return lt_v, lt_T

    def emit_ln_pass(scratch_off, wait_extra=()):
        lt = None
        last_tr_T = 0
        for i in range(NT):
            ew = [("pT", last_tr_T)] if last_tr_T else []
            t_u, _, _ = emit_ln_tile(x_sb[:, i * 1024:(i + 1) * 1024],
                                     Cc[:, scratch_off:scratch_off + 1024],
                                     stat_off=16 * (i % 2),
                                     extra_wait=list(wait_extra) if i == 0 else (),
                                     apply_wait=ew)
            lt, last_tr_T = emit_transpose_1024(
                lambda j: Cc[:, scratch_off + 128 * j:scratch_off + 128 * (j + 1)],
                lambda j, i=i: A[:, j * STRIDE + 1 + i * 128:j * STRIDE + 1 + (i + 1) * 128],
                "ptr", wait_src=[("pS", t_u)])
        return lt, last_tr_T

    def pkx(i):
        return pk6 if i == 6 else (pk7 if i == 7 else pk[i])

    def emit_attn_a():
        QV, KV_, VV = 0, 2048, 4096
        PBUF = 6144
        SCR = 12288
        VT, WIPO, PT = 0, 4160, 10304

        lt, ltT = emit_ln_pass_batched(
            list(range(NT)),
            lambda ii, j: A[:, j * STRIDE + 1 + ii * 128:j * STRIDE + 1 + (ii + 1) * 128],
            SCR,
            tile_waits=(lambda ii: [xload_toks[ii]]) if xload_toks else None)
        ln_done = [("pV", lt), ("pT", ltT)]

        ones_tok = em.op("V", lambda v: v.memset(Dd[:, 0:4160], 1.0))

        wipT_r = D["wipT"].rearrange("(k p) f -> p k f", p=128)

        def dma_wip(hp, slot):
            g = wip_guard.get(slot)
            if g:
                em.waits("G", [("pT", g)])
            toks = []
            for o in range(3):
                col0 = 1024 * o + 128 * hp
                dst = Dd[:, WIPO + slot * 3072 + o * 1024:WIPO + slot * 3072 + (o + 1) * 1024]
                dst = dst.rearrange("p (k f) -> p k f", f=128)
                toks.append(em.dma("G", dst, wipT_r[:, :, col0:col0 + 128],
                                   ch="A" if slot == 0 else "B"))
            return toks[-1]

        wip_tok = {}
        wip_guard = {}
        wip_tok[0] = dma_wip(0, 0)

        attn_done_tok = 0
        sband_guard = {}
        pair_q_guard = {}
        c0_done = {}
        negc0 = {}
        f_ptr = 34

        for hp in range(8):
            slot = hp % 2
            base = PBUF * slot
            if hp + 1 < 8:
                wip_tok[hp + 1] = dma_wip(hp + 1, (hp + 1) % 2)
            ev_tok = None
            prev_use = pair_q_guard.get(('qk', slot), 0)
            for o in range(3):
                for n in range(4):
                    bi = 6 + (o * 4 + n) % 2
                    w0 = []
                    if o == 0 and n == 0:
                        w0 += ln_done + [wip_tok[hp]]
                        if prev_use:
                            w0.append(("pT", prev_use))
                    g = sband_guard.get(bi)
                    if g:
                        w0.append(g)
                    mm = None
                    for k in range(NC):
                        mm = em.op("T", lambda t, o=o, n=n, k=k, bi=bi, slot=slot:
                                   t.matmul(pkx(bi)[:, 0:512],
                                            Dd[:, WIPO + slot * 3072 + o * 1024 + k * 128:
                                               WIPO + slot * 3072 + o * 1024 + (k + 1) * 128],
                                            A[:, k * STRIDE + 1 + n * 512:k * STRIDE + 1 + (n + 1) * 512],
                                            start=(k == 0), stop=(k == NC - 1)),
                                   waits=(w0 if k == 0 else []), chain=False)
                    dst = Cc[:, base + o * 2048 + n * 512:base + o * 2048 + (n + 1) * 512]
                    if o == 2:
                        ev_tok = em.op("S", lambda s, dst=dst, bi=bi:
                                       s.copy(dst, pkx(bi)[:, 0:512]),
                                       waits=[("pT", mm)])
                    else:
                        ev_tok = em.op("S", lambda s, dst=dst, bi=bi:
                                       s.activation(dst, pkx(bi)[:, 0:512], AF.Identity,
                                                    scale=0.3535533905932738),
                                       waits=[("pT", mm)])
                    sband_guard[bi] = ("pS", ev_tok)
            qkv_done = ("pS", ev_tok)
            wip_guard[slot] = em.last["T"]

            fb = f_ptr
            b1 = em.op("V", lambda v, base=base, fb=fb:
                       v.bn_stats(f32s[:, fb:fb + 6], Cc[:, base + QV:base + QV + 512]),
                       waits=[qkv_done])
            b2 = em.op("V", lambda v, fb=fb:
                       v.bn_aggr(f32s[:, fb + 12:fb + 14], f32s[:, fb:fb + 6]))
            b3 = em.op("V", lambda v, base=base, fb=fb:
                       v.bn_stats(f32s[:, fb + 6:fb + 12], Cc[:, base + KV_:base + KV_ + 512]))
            b4 = em.op("V", lambda v, fb=fb:
                       v.bn_aggr(f32s[:, fb + 14:fb + 16], f32s[:, fb + 6:fb + 12]))
            b5 = em.op("V", lambda v, fb=fb:
                       v.scalar_tensor_tensor(f32s[:, fb + 16:fb + 17],
                                              f32s[:, fb + 12:fb + 13],
                                              f32s[:, fb + 12:fb + 13],
                                              f32s[:, fb + 13:fb + 14],
                                              ALU.mult, ALU.add))
            b6 = em.op("V", lambda v, fb=fb:
                       v.scalar_tensor_tensor(f32s[:, fb + 17:fb + 18],
                                              f32s[:, fb + 14:fb + 15],
                                              f32s[:, fb + 14:fb + 15],
                                              f32s[:, fb + 15:fb + 16],
                                              ALU.mult, ALU.add))
            c0g = sband_guard.get("c0")
            prev_e2 = None
            for h2 in range(2):
                head = 2 * hp + h2
                rows = slice(64 * h2, 64 * h2 + 64)
                half_ap = (cs["c_half64"][0:64, :] if h2 == 0
                           else cs["c_half64_hi"][64:128, :])
                av2g = sband_guard.get(("av", 0))
                mm1 = em.op("T", lambda t, fb=fb, rows=rows, half_ap=half_ap:
                            t.matmul(pk[2][0:1, 0:1], f32s[rows, fb + 16:fb + 17],
                                     half_ap, start=True, stop=False),
                            waits=[("pV", b6)] + ([c0g] if c0g else []) +
                                  ([av2g] if av2g else []) +
                                  ([("pV", prev_e2)] if prev_e2 else []), chain=False)
                mm2 = em.op("T", lambda t, fb=fb, rows=rows, half_ap=half_ap:
                            t.matmul(pk[2][0:1, 0:1], f32s[rows, fb + 17:fb + 18],
                                     half_ap, start=False, stop=True),
                            chain=False)
                e1 = em.op("S", lambda s, fb=fb, h2=h2:
                           s.activation(f32s[0:1, fb + 18 + h2:fb + 19 + h2],
                                        pk[2][0:1, 0:1], AF.Identity,
                                        bias=negmargin_c[0:1, :], scale=-1.0),
                           waits=[("pT", mm2)])
                mm3 = em.op("T", lambda t, fb=fb, h2=h2:
                            t.matmul(pk[2][0:128, 2 + h2:3 + h2], cs["c_ones_row"][:],
                                     f32s[0:1, fb + 18 + h2:fb + 19 + h2],
                                     start=True, stop=True),
                            waits=[("pS", e1)], chain=False)
                e2 = em.op("V", lambda v, fb=fb, h2=h2:
                           v.tensor_copy(f32s[:, fb + 20 + h2:fb + 21 + h2],
                                         pk[2][0:128, 2 + h2:3 + h2]),
                           waits=[("pT", mm3)])
                negc0[head] = (f32s[:, fb + 20 + h2:fb + 21 + h2], e2)
                prev_e2 = e2
                c0g = ("pV", e2)
            sband_guard["c0"] = c0g
            c0_last = ("pV", em.last["V"])
            f_ptr = 34 if f_ptr == 70 else 70

            vt_dst = VT + slot * 2080
            vtrT = 0
            for rr in range(4):
                bank = pk[4 + rr % 2]
                g = sband_guard.get(4 + rr % 2)
                tk = []
                for j in range(4):
                    c = rr * 4 + j
                    w = []
                    if j == 0:
                        w = [qkv_done] if rr == 0 else []
                        if rr == 0:
                            w.append(("pV", ones_tok))
                        if g:
                            w.append(g)
                    tk.append(em.op("T", lambda t, c=c, j=j, bank=bank, base=base:
                                    t.matmul(bank[:, 128 * j:128 * (j + 1)],
                                             Cc[:, base + VV + c * 128:base + VV + (c + 1) * 128],
                                             cs["c_ident"][:], start=True, stop=True),
                                    waits=w, chain=False))
                vtrT = tk[-1]
                evA = em.op("V", lambda v, rr=rr, bank=bank, vt_dst=vt_dst:
                            v.tensor_copy(
                                Dd[:, vt_dst + rr * 4 * 65:vt_dst + (rr * 4 + 4) * 65]
                                .rearrange("p (c f) -> p c f", f=65)[:, :, 0:64],
                                bank[:].rearrange("p (c f) -> p c f", f=128)[:, :, 0:64]),
                            waits=[("pT", tk[-1])])
                evB = em.op("V", lambda v, rr=rr, bank=bank, vt_dst=vt_dst:
                            v.tensor_copy(
                                Dd[:, vt_dst + 1040 + rr * 4 * 65:vt_dst + 1040 + (rr * 4 + 4) * 65]
                                .rearrange("p (c f) -> p c f", f=65)[:, :, 0:64],
                                bank[:].rearrange("p (c f) -> p c f", f=128)[:, :, 64:128]),
                            )
                sband_guard[4 + rr % 2] = ("pV", evB)
            vtok_done = ("pV", em.last["V"])

            PT4 = 10304
            sbmap = ((0, 6), (1, 7))
            for qc in range(4):
                nkc = 4 * (qc + 1)
                for h2 in range(2):
                    g = sband_guard.get(("av", h2))
                    w = [vtok_done]
                    if g:
                        w.append(g)
                    sband_guard[("avw", h2)] = w
                for kc in range(nkc + 1):
                    if kc < nkc:
                        qk_toks = {}
                        for h2 in range(2):
                            head = 2 * hp + h2
                            rows = slice(64 * h2, 64 * h2 + 64)
                            sbank = sbmap[h2][kc % 2]
                            gs = sband_guard.get(("s", sbank))
                            wqk = [qkv_done] if (qc == 0 and kc == 0) else []
                            if gs:
                                wqk.append(gs)
                            qk_toks[h2] = em.op("T", lambda t, kc=kc, qc=qc, sbank=sbank, base=base, rows=rows:
                                                t.matmul(pkx(sbank)[:, 0:512],
                                                         Cc[rows, base + KV_ + kc * 128:base + KV_ + (kc + 1) * 128],
                                                         Cc[rows, base + QV + qc * 512:base + QV + (qc + 1) * 512],
                                                         start=True, stop=True),
                                                waits=wqk, chain=False)
                        for h2 in range(2):
                            head = 2 * hp + h2
                            sbank = sbmap[h2][kc % 2]
                            pb = PT4 + (h2 * 2 + kc % 2) * 512
                            nc0_ap, nc0_tok = negc0[head]
                            we = [("pT", qk_toks[h2]), ("pV", nc0_tok)]
                            gp = sband_guard.get(("pt", h2, kc % 2))
                            if gp:
                                we.append(gp)
                            ex = em.op("S", lambda s, pb=pb, sbank=sbank, nc0_ap=nc0_ap:
                                       s.activation(Dd[:, pb:pb + 512], pkx(sbank)[:, 0:512],
                                                    AF.Exp, bias=nc0_ap, scale=1.0),
                                       waits=we)
                            sband_guard[("s", sbank)] = ("pS", ex)
                            p_ready = ("pS", ex)
                            m = kc - 4 * qc
                            if m >= 0:
                                mk = em.op("V", lambda v, pb=pb, m=m:
                                           v.tensor_mul(Dd[:, pb:pb + 512], Dd[:, pb:pb + 512],
                                                        cs["c_dband"][:, 384 - 128 * m:896 - 128 * m]),
                                           waits=[("pS", ex)])
                                p_ready = ("pV", mk)
                            sband_guard[("pr", h2, kc % 2)] = p_ready
                    if kc >= 1:
                        kcp = kc - 1
                        for h2 in range(2):
                            pb = PT4 + (h2 * 2 + kcp % 2) * 512
                            vt_base = vt_dst + 1040 * h2
                            av_mm = None
                            for sq in range(4):
                                w = []
                                if sq == 0:
                                    w = [sband_guard[("pr", h2, kcp % 2)]]
                                    if kcp == 0:
                                        w += sband_guard[("avw", h2)]
                                av_mm = em.op("T", lambda t, sq=sq, kcp=kcp, pb=pb, h2=h2, vt_base=vt_base, nkc=nkc:
                                              t.matmul(pk[2 + h2][:, 65 * sq:65 * sq + 65],
                                                       Dd[:, pb + 128 * sq:pb + 128 * (sq + 1)],
                                                       Dd[:, vt_base + 65 * kcp:vt_base + 65 * (kcp + 1)],
                                                       start=(kcp == 0 and sq == 0),
                                                       stop=(kcp == nkc - 1 and sq == 3)),
                                              waits=w, chain=False)
                            sband_guard[("pt", h2, kcp % 2)] = ("pT", av_mm)
                            sband_guard[("avl", h2)] = ("pT", av_mm)
                for h2 in range(2):
                    head = 2 * hp + h2
                    rc = em.op("V", lambda v, h2=h2: v.reciprocal(
                        f32s[:, 24 + 4 * h2:28 + 4 * h2],
                        pk[2 + h2][:, 0:260].rearrange("p (s f) -> p s f", f=65)[:, :, 64:65]),
                        waits=[sband_guard[("avl", h2)]])
                    last_ep = rc
                    for sq in range(4):
                        i_tok = qc * 4 + sq
                        last_ep = em.op("V", lambda v, sq=sq, h2=h2, i_tok=i_tok, head=head:
                                        v.tensor_scalar(
                                            Bb[:, i_tok * 1024 + 64 * head:i_tok * 1024 + 64 * head + 64],
                                            pk[2 + h2][:, 65 * sq:65 * sq + 64],
                                            f32s[:, 24 + 4 * h2 + sq:25 + 4 * h2 + sq], None, ALU.mult))
                    sband_guard[("av", h2)] = ("pV", last_ep)
            pair_q_guard[("qk", slot)] = em.last["T"]
        attn_done = [("pV", em.last["V"]), ("pT", em.last["T"]), ("pS", em.last["S"])]

        lt2 = None
        ltT2 = 0
        for i in range(NT):
            lt2, ltT2 = emit_transpose_1024(
                lambda j, i=i: Bb[:, i * 1024 + 128 * j:i * 1024 + 128 * (j + 1)],
                lambda j, i=i: A[:, j * STRIDE + 1 + i * 128:j * STRIDE + 1 + (i + 1) * 128],
                "ptr", wait_src=(attn_done if i == 0 else []))
        atr_done = [("pV", lt2), ("pT", ltT2)]

        em.waits("G", [("pT", em.last["T"]), ("pS", em.last["S"]), ("pV", em.last["V"])])
        wo_tok = None
        for k in range(NC):
            dst = Dd[:, k * 1024:(k + 1) * 1024]
            wo_tok = em.dma("G", dst, D["woaT"][128 * k:128 * (k + 1), :], ch="W")
        wt = None
        for k in range(NC):
            wt = em.dma("G", Dd[:, 8192 + k * 1024:8192 + (k + 1) * 1024],
                        D["wkT"][128 * k:128 * (k + 1), :], ch="W")
        wpf["wk"] = wt
        for k in range(NC):
            wt = em.dma("G", Cc[:, 8192 + k * 1024:8192 + (k + 1) * 1024],
                        D["wvT"][128 * k:128 * (k + 1), :], ch="W")
        wpf["wv"] = wt
        o_guard = {}
        for ii in range(NT):
            for fc in range(2):
                bi = (ii * 2 + fc) % 2
                w0 = []
                if ii == 0 and fc == 0:
                    w0 += atr_done + [wo_tok]
                g = o_guard.get(bi)
                if g:
                    w0.append(("pV", g))
                mm = None
                for k in range(NC):
                    mm = em.op("T", lambda t, k=k, ii=ii, fc=fc, bi=bi:
                               t.matmul(pk[bi][:, 0:512],
                                        A[:, k * STRIDE + 1 + ii * 128:k * STRIDE + 1 + (ii + 1) * 128],
                                        Dd[:, k * 1024 + fc * 512:k * 1024 + (fc + 1) * 512],
                                        start=(k == 0), stop=(k == NC - 1)),
                               waits=(w0 if k == 0 else []), chain=False)
                o_guard[bi] = em.op("V", lambda v, ii=ii, fc=fc, bi=bi:
                                    v.tensor_add(x_sb[:, ii * 1024 + fc * 512:ii * 1024 + (fc + 1) * 512],
                                                 x_sb[:, ii * 1024 + fc * 512:ii * 1024 + (fc + 1) * 512],
                                                 pk[bi][:, 0:512]),
                                    waits=[("pT", mm)])
        lastmm = em.last["T"]
        em.waits("G", [("pT", lastmm)])
        for k in range(NC):
            wt = em.dma("G", Dd[:, k * 1024:(k + 1) * 1024],
                        D["wrT"][128 * k:128 * (k + 1), :], ch="W")
        wpf["wr"] = wt
        em.waits("V", [("pT", lastmm)])
        em.waits("S", [("pT", lastmm)])

    def emit_ffn():
        w_r = lambda k, f0: Dd[:, k * 1024 + f0:k * 1024 + f0 + 512]
        w_k = lambda k, fb: Dd[:, 8192 + k * 1024 + fb * 128:8192 + k * 1024 + (fb + 1) * 128]
        w_v = lambda k, f0: Cc[:, 8192 + k * 1024 + f0:8192 + k * 1024 + f0 + 512]
        if "wr" in wpf:
            wr_tok = wpf.pop("wr")
            wk_tok = wpf.pop("wk")
            wv_tok = wpf.pop("wv")
        else:
            wt = None
            for k in range(NC):
                wt = em.dma("G", Dd[:, k * 1024:(k + 1) * 1024],
                            D["wrT"][128 * k:128 * (k + 1), :], ch="W")
            wr_tok = wt
            for k in range(NC):
                wt = em.dma("G", Dd[:, 8192 + k * 1024:8192 + (k + 1) * 1024],
                            D["wkT"][128 * k:128 * (k + 1), :], ch="W")
            wk_tok = wt
            for k in range(NC):
                wt = em.dma("G", Cc[:, 8192 + k * 1024:8192 + (k + 1) * 1024],
                            D["wvT"][128 * k:128 * (k + 1), :], ch="W")
            wv_tok = wt
        w_ready = [wr_tok]
        if ffn_stage <= 0.2:
            em.barrier()
            return

        for k in range(NC):
            em.op("V", lambda v, k=k: v.memset(A[:, k * STRIDE:k * STRIDE + 1], 0.0))
        if ffn_stage <= 0.4:
            em.barrier()
            return
        lt, last_tr_T = emit_ln_pass_batched(
            list(range(NT)),
            lambda ii, j: A[:, j * STRIDE + 1 + ii * 128:j * STRIDE + 1 + (ii + 1) * 128],
            0)
        u_done = [("pV", lt), ("pT", last_tr_T)]
        if ffn_stage <= 1:
            em.barrier()
            return

        r_banks = (0, 1)
        ka_banks = (2, 3)
        o_banks = (4, 5)
        r_ev = {}
        ka_ev = {}
        o_ev = {}
        for n in range(4):
            for k in range(NC):
                u_ap = A[:, k * STRIDE + 1 + n * 512:k * STRIDE + 1 + n * 512 + 512]
                up_ap = A[:, k * STRIDE + 0 + n * 512:k * STRIDE + 0 + n * 512 + 512]
                dl = Cc[:, k * 512:(k + 1) * 512]
                xk_ap = Cc[:, 4096 + k * 512:4608 + k * 512]
                em.op("V", lambda v, u_ap=u_ap, up_ap=up_ap, dl=dl: v.tensor_sub(dl, u_ap, up_ap),
                      waits=(u_done if (n == 0 and k == 0) else []))
                em.op("V", lambda v, dl=dl, up_ap=up_ap, xk_ap=xk_ap, k=k:
                      v.scalar_tensor_tensor(xk_ap, dl, pcols["muk_col"][:, k:k + 1],
                                             up_ap, ALU.mult, ALU.add))
                em.op("V", lambda v, dl=dl, up_ap=up_ap, k=k:
                      v.scalar_tensor_tensor(dl, dl, pcols["mur_col"][:, k:k + 1],
                                             up_ap, ALU.mult, ALU.add))
            mixed = [("pV", em.last["V"])]
            if ffn_stage <= 2:
                continue

            for ii in range(4):
                for fc in range(2):
                    bi = r_banks[(ii * 2 + fc) % 2]
                    w0 = list(mixed)
                    if n == 0 and ii == 0 and fc == 0:
                        w0 += w_ready
                    if bi in r_ev:
                        w0.append(("pS", r_ev[bi]))
                    mm = None
                    for k in range(NC):
                        mm = em.op("T", lambda t, k=k, ii=ii, fc=fc, bi=bi:
                                   t.matmul(pk[bi][:, 0:512],
                                            Cc[:, k * 512 + ii * 128:k * 512 + (ii + 1) * 128],
                                            w_r(k, fc * 512),
                                            start=(k == 0), stop=(k == NC - 1)),
                                   waits=(w0 if k == 0 else []), chain=False)
                    r_ev[bi] = em.op("S", lambda s, bi=bi, ii=ii, fc=fc:
                                     s.activation(Bb[:, ii * 1024 + fc * 512:ii * 1024 + fc * 512 + 512],
                                                  pk[bi][:, 0:512], AF.Sigmoid),
                                     waits=[("pT", mm)])
            r_done = ("pS", em.last["S"])
            if ffn_stage <= 3:
                continue

            for fb in range(NC):
                bi = ka_banks[fb % 2]
                w0 = list(mixed)
                if n == 0 and fb == 0:
                    w0.append(wk_tok)
                if bi in ka_ev:
                    w0.append(("pS", ka_ev[bi]))
                mm = None
                for k in range(NC):
                    mm = em.op("T", lambda t, k=k, fb=fb, bi=bi:
                               t.matmul(pk[bi][:, 0:512], w_k(k, fb),
                                        Cc[:, 4096 + k * 512:4608 + k * 512],
                                        start=(k == 0), stop=(k == NC - 1)),
                               waits=(w0 if k == 0 else []), chain=False)
                ka_ev[bi] = em.op("S", lambda s, bi=bi, fb=fb:
                                  s.activation(Bb[:, 4096 + fb * 512:4608 + fb * 512],
                                               pk[bi][:, 0:512], AF.Relu),
                                  waits=[("pT", mm)])
                em.op("V", lambda v, fb=fb:
                      v.tensor_mul(Bb[:, 4096 + fb * 512:4608 + fb * 512],
                                   Bb[:, 4096 + fb * 512:4608 + fb * 512],
                                   Bb[:, 4096 + fb * 512:4608 + fb * 512]),
                      waits=[("pS", ka_ev[bi])])
            ka_done = ("pV", em.last["V"])
            if ffn_stage <= 4:
                continue

            for ii in range(4):
                i = 4 * n + ii
                for dc in range(2):
                    bi = o_banks[(ii * 2 + dc) % 2]
                    w0 = [ka_done]
                    if n == 0 and ii == 0 and dc == 0:
                        w0.append(wv_tok)
                    if bi in o_ev:
                        w0.append(("pV", o_ev[bi]))
                    mm = None
                    for k in range(NC):
                        mm = em.op("T", lambda t, k=k, ii=ii, dc=dc, bi=bi:
                                   t.matmul(pk[bi][:, 0:512],
                                            Bb[:, 4096 + k * 512 + ii * 128:4096 + k * 512 + (ii + 1) * 128],
                                            w_v(k, dc * 512),
                                            start=(k == 0), stop=(k == NC - 1)),
                                   waits=(w0 if k == 0 else []), chain=False)
                    t1 = em.op("V", lambda v, bi=bi, ii=ii, dc=dc:
                               v.tensor_mul(f32s[:, 288:800],
                                            pk[bi][:, 0:512],
                                            Bb[:, ii * 1024 + dc * 512:ii * 1024 + dc * 512 + 512]),
                               waits=[("pT", mm), r_done])
                    o_ev[bi] = t1
                    em.op("V", lambda v, i=i, dc=dc:
                          v.tensor_add(x_sb[:, i * 1024 + dc * 512:i * 1024 + dc * 512 + 512],
                                       x_sb[:, i * 1024 + dc * 512:i * 1024 + dc * 512 + 512],
                                       f32s[:, 288:800]))
        em.barrier()


    def emit_gb_rebuild(g_name, b_name, stage_off, gate):
        em.waits("G", gate)
        em.dma("G", Dd[0:1, stage_off:stage_off + 1024], D[g_name], ch="W")
        d2 = em.dma("G", Dd[0:1, stage_off + 1024:stage_off + 2048], D[b_name], ch="W")
        tok = None
        prev = {}
        for i, dst in enumerate([g_bc, b_bc]):
            for h2 in range(2):
                bi = (i * 2 + h2) % 2
                w = [d2] + list(gate)
                if bi in prev:
                    w.append(("pV", prev[bi]))
                mm = em.op("T", lambda t, i=i, h2=h2, bi=bi, stage_off=stage_off: t.matmul(
                    pk[bi][:, 0:512], cs["c_ones_row_bf"][:],
                    Dd[0:1, stage_off + i * 1024 + h2 * 512:stage_off + i * 1024 + h2 * 512 + 512],
                    start=True, stop=True), waits=w, chain=False)
                tok = em.op("V", lambda v, dst=dst, h2=h2, bi=bi:
                            v.tensor_copy(dst[:, h2 * 512:h2 * 512 + 512], pk[bi][:, 0:512]),
                            waits=[("pT", mm)])
                prev[bi] = tok
        return ("pV", tok)

    def emit_attn_b():
        STM, STB, QTOK, KTOK, VS = 0, 2048, 4096, 5120, 6144
        KVEC, QVEC = 7168, 7200
        SICOL, SOCOL, ECOL, CKCQ, CS2 = 7232, 7248, 7264, 7280, 7312
        KST3, QST3, SROW = 7344, 7392, 7440
        AKV = 1026
        UCM, ATK = 8192, 8192
        pPACD, pSCAN, pOUT, pKV, pSM = pk[0], pk[1], pk[2], pk[3], pk6
        pOUT2 = pk7
        FB, FC = 400, 448
        EV1, EV2 = 608, 768

        def ecol(c0):
            return f32s[:, EV1:EV1 + 160].rearrange("p (h c) -> p h c", c=10)[:, :, c0]
        ec2 = f32s[:, EV2:EV2 + 32].rearrange("p (h c) -> p h c", c=2)

        guards = {}

        em.op("V", lambda v: v.memset(A[:, AKV:AKV + 1024], 0.0))
        for base in (KST3, QST3):
            em.op("V", lambda v, base=base: v.memset(Dd[:, base:base + 48], 0.0))
            em.op("V", lambda v, base=base: v.memset(
                Dd[0:64, base:base + 48].rearrange("p (h c) -> p h c", c=6)[:, :, 1], 1.0))
            em.op("V", lambda v, base=base: v.memset(
                Dd[64:128, base:base + 48].rearrange("p (h c) -> p h c", c=6)[:, :, 4], 1.0))
        em.op("V", lambda v: v.memset(Dd[0:1, SROW:SROW + 80], 0.0))
        em.op("V", lambda v: v.memset(Dd[:, KVEC:KVEC + 32], 1.0))
        em.op("V", lambda v: v.memset(Dd[:, QVEC:QVEC + 32], 1.0))
        em.op("V", lambda v: v.memset(
            Dd[:, KVEC:KVEC + 32].rearrange("p (h c) -> p h c", c=2)[:, :, 1], 0.0))
        em.op("V", lambda v: v.memset(
            Dd[:, QVEC:QVEC + 32].rearrange("p (h c) -> p h c", c=2)[:, :, 1], 0.0))
        st_init = em.op("V", lambda v: v.memset(Dd[:, VS:VS + 512], 0.0))
        init_done = [("pV", st_init)]

        def qslc(h, lt):
            p, h2 = h // 2, h % 2
            return Bb[64 * h2:64 * h2 + 64, p * 1024 + lt * 128:p * 1024 + (lt + 1) * 128]

        def kslc(h, lt):
            p, h2 = h // 2, h % 2
            return Bb[64 * h2:64 * h2 + 64, 8192 + p * 1024 + lt * 128:8192 + p * 1024 + (lt + 1) * 128]

        def stm(h):
            return Dd[:, STM + h * 128:STM + (h + 1) * 128]

        def stb(h):
            return Dd[:, STB + h * 128:STB + (h + 1) * 128]


        def emit_ln_half(half):
            lt_v, lt_T = emit_ln_pass_batched(
                list(range(8 * half, 8 * half + 8)),
                lambda ii, j: Cc[:, UCM + j * 1024 + (ii - 8 * half) * 128:
                                 UCM + j * 1024 + (ii - 8 * half + 1) * 128],
                0, gdict=guards)
            return [("pV", lt_v), ("pT", lt_T)]


        def proj_qk(which, dst_base, bias_col, u_done_):
            wname = "wq" if which == "q" else "wk"
            wcs = D[wname].rearrange("(k p) f -> p k f", p=128)
            em.waits("G", [w for w in u_done_ if w[0] in ("pT", "pV", "pS")])
            evs = None
            for fb in range(NC):
                slot = fb % 2
                g = guards.get(("pw", slot))
                if g:
                    em.waits("G", [g])
                dt_ = em.dma("G", Dd[:, STM + slot * 1024:STM + (slot + 1) * 1024]
                             .rearrange("p (k f) -> p k f", f=128),
                             wcs[:, :, 128 * fb:128 * (fb + 1)],
                             ch=("A" if slot == 0 else "B"))
                last_mm = None
                for tc in range(2):
                    bi = 4 + (fb * 2 + tc) % 2
                    gb = guards.get(bi)
                    w0 = [dt_] + list(u_done_)
                    if gb:
                        w0.append(gb)
                    mm = None
                    for k in range(NC):
                        mm = em.op("T", lambda t, k=k, tc=tc, bi=bi, slot=slot:
                                   t.matmul(pk[bi][:, 0:512],
                                            Dd[:, STM + slot * 1024 + k * 128:STM + slot * 1024 + (k + 1) * 128],
                                            Cc[:, UCM + k * 1024 + tc * 512:UCM + k * 1024 + (tc + 1) * 512],
                                            start=(k == 0), stop=(k == NC - 1)),
                                   waits=(w0 if k == 0 else []), chain=False)
                    evs = em.op("S", lambda s, fb=fb, tc=tc, bi=bi, dst_base=dst_base, bias_col=bias_col:
                                s.activation(Bb[:, dst_base + fb * 1024 + tc * 512:dst_base + fb * 1024 + (tc + 1) * 512],
                                             pk[bi][:, 0:512], AF.Sigmoid,
                                             bias=bias_col[:, fb:fb + 1], scale=1.0),
                                waits=[("pT", mm)])
                    guards[bi] = ("pS", evs)
                    last_mm = mm
                guards[("pw", slot)] = ("pT", last_mm)
            return ("pS", evs)

        def proj_v(u_done_):
            em.waits("G", [w for w in u_done_ if w[0] in ("pT", "pV", "pS")])
            for sl in (0, 1):
                g = guards.get(("pw", sl))
                if g:
                    em.waits("G", [g])
            bv_d = em.dma("G", Dd[0:1, ATK:ATK + 1024], D["bv_row"], ch="W")
            evs = None
            for dc in range(2):
                g = guards.get(("pv", dc))
                if g:
                    em.waits("G", [g])
                vst = 0 if dc == 0 else ATK + 1024
                dt_ = em.dma("G", Dd[:, vst:vst + 4096]
                             .rearrange("p (k f) -> p k f", f=512),
                             D["wv"].rearrange("(k p) f -> p k f", p=128)[:, :, dc * 512:(dc + 1) * 512],
                             ch=("A" if dc == 0 else "B"))
                last_mm = None
                for tt in range(8):
                    bi = 4 + (dc * 8 + tt) % 2
                    gb = guards.get(bi)
                    w0 = [dt_, bv_d] + list(u_done_)
                    if gb:
                        w0.append(gb)
                    mm = None
                    for k in range(NC):
                        mm = em.op("T", lambda t, k=k, tt=tt, dc=dc, bi=bi,
                                   vst=(0 if dc == 0 else ATK + 1024):
                                   t.matmul(pk[bi][:, 0:512],
                                            Cc[:, UCM + k * 1024 + tt * 128:UCM + k * 1024 + (tt + 1) * 128],
                                            Dd[:, vst + k * 512:vst + (k + 1) * 512],
                                            start=(k == 0), stop=False),
                                   waits=(w0 if k == 0 else []), chain=False)
                    mm = em.op("T", lambda t, dc=dc, bi=bi:
                               t.matmul(pk[bi][:, 0:512], cs["c_ones_row_bf"][:],
                                        Dd[0:1, ATK + dc * 512:ATK + (dc + 1) * 512],
                                        start=False, stop=True), chain=False)
                    evs = em.op("S", lambda s, tt=tt, dc=dc, bi=bi:
                                s.copy(Cc[:, tt * 1024 + dc * 512:tt * 1024 + (dc + 1) * 512],
                                       pk[bi][:, 0:512]),
                                waits=[("pT", mm)])
                    guards[bi] = ("pS", evs)
                    last_mm = mm
                guards[("pv", dc)] = ("pT", last_mm)
            return ("pS", evs)


        def scan_chunk(gc, qkv_done):
            lt = gc % 8
            lp_col = cs["c_lpos"][:, gc:gc + 1]
            rlp_col = cs["c_rlpos"][:, gc:gc + 1]
            if gc == 0:
                ow = [w for w in [guards.get("SMrd"), guards.get("KVrd")] if w]
                em.op("T", lambda t: t.matmul(pKV[:, 0:512], cs["c_ident"][:],
                                              Dd[:, VS:VS + 512], start=True,
                                              stop=False, skip_group_check=True),
                      waits=ow + init_done + list(qkv_done), chain=False)
                op2 = em.op("T", lambda t: t.matmul(pSM[:, 0:512], cs["c_ident"][:],
                                                    Dd[:, VS:VS + 512], start=True,
                                                    stop=False, skip_group_check=True),
                            chain=False)
                guards["VSb"] = ("pT", op2)

            def pairt(h, off=0):
                p = h // 2
                return Bb[:, off + p * 1024 + lt * 128:off + p * 1024 + (lt + 1) * 128]

            tr_ev = None
            for grp in range(4):
                bank = pk[4 + grp % 2]
                g = guards.get(4 + grp % 2)
                tk = None
                for j in range(4):
                    idx = grp * 4 + j
                    p = idx % 8
                    off = 0 if idx < 8 else 8192
                    src = Bb[:, off + p * 1024 + lt * 128:off + p * 1024 + (lt + 1) * 128]
                    w = []
                    if j == 0:
                        w = list(qkv_done) + init_done if grp == 0 else []
                        if g:
                            w.append(g)
                    tk = em.op("T", lambda t, src=src, j=j, bank=bank:
                               t.matmul(bank[:, 128 * j:128 * (j + 1)], src,
                                        cs["c_ident"][:], start=True, stop=True),
                               waits=w, chain=False)
                tr_ev = em.op("V", lambda v, grp=grp, bank=bank:
                              v.tensor_copy(Dd[:, QTOK + grp * 512:QTOK + (grp + 1) * 512],
                                            bank[:, 0:512]),
                              waits=[("pT", tk)])
                guards[4 + grp % 2] = ("pV", tr_ev)
            tr_done = ("pV", tr_ev)
            if DBG_SCAN == "tr":
                return

            stm_ev = None
            for grp in range(8):
                isST = grp < 4
                par = grp % 2
                half4 = (grp // 2) % 2
                bank = pk[4 + par]
                g = guards.get(4 + par)
                heads = [par + 2 * (4 * half4 + j) for j in range(4)]
                tk = None
                for j, h in enumerate(heads):
                    r0 = 64 * par
                    lh = (kslc(h, lt) if isST else qslc(h, lt))
                    rh = (qslc(h, lt) if isST else kslc(h, lt))
                    w = []
                    if j == 0:
                        w = [tr_done] if grp == 0 else []
                        if g:
                            w.append(g)
                    tk = em.op("T", lambda t, lh=lh, rh=rh, j=j, bank=bank:
                               t.matmul(bank[:, 128 * j:128 * (j + 1)], lh, rh,
                                        start=True, stop=True),
                               waits=w, chain=False)
                dst0 = STM if isST else STB
                h0 = par + 8 * half4
                stm_ev = em.op("V", lambda v, h0=h0, bank=bank, dst0=dst0:
                               v.tensor_mul(
                                   Dd[:, dst0:dst0 + 2048]
                                   .rearrange("p (h c) -> p h c", c=128)[:, h0:min(h0 + 8, 16):2, :],
                                   bank[:, 0:512].rearrange("p (j c) -> p j c", c=128),
                                   cs["c_triu4"][:].rearrange("p (j c) -> p j c", c=128)),
                               waits=[("pT", tk)])
                guards[4 + par] = ("pV", stm_ev)
            stm_done = ("pV", stm_ev)
            if DBG_SCAN == "st":
                return

            gA = guards.get("PACD")
            gsr = guards.get("SMrd")
            mmA = None
            for h in range(16):
                w = [stm_done]
                if h == 0:
                    if gA:
                        w.append(gA)
                    if gsr:
                        w.append(gsr)
                mmA = em.op("T", lambda t, h=h, lt=lt:
                            t.matmul(pPACD[:, 10 * h + 4:10 * h + 7], pairt(h),
                                     Dd[:, KST3 + 3 * h:KST3 + 3 * h + 3],
                                     start=(h == 0), stop=False, skip_group_check=True),
                            waits=w, chain=False)
                mmA = em.op("T", lambda t, h=h, lt=lt:
                            t.matmul(pPACD[:, 10 * h + 7:10 * h + 10], pairt(h, 8192),
                                     Dd[:, QST3 + 3 * h:QST3 + 3 * h + 3],
                                     start=False, stop=False, skip_group_check=True),
                            chain=False)
                mmA = em.op("T", lambda t, h=h:
                            t.matmul(pPACD[:, 10 * h:10 * h + 1], stm(h), cs["c_ones_col"][:],
                                     start=False, stop=False, skip_group_check=True),
                            chain=False)
                mmA = em.op("T", lambda t, h=h:
                            t.matmul(pPACD[:, 10 * h + 1:10 * h + 2], stb(h), cs["c_ones_col"][:],
                                     start=False, stop=False, skip_group_check=True),
                            chain=False)
            phA = ("pT", mmA)
            if DBG_SCAN == "phA":
                return

            def pcol(c0):
                return pPACD[:, 0:160].rearrange("p (h c) -> p h c", c=10)[:, :, c0]

            cpk = em.op("V", lambda v: v.tensor_copy(Dd[:, CKCQ:CKCQ + 16], pcol(8)),
                        waits=[phA])
            cpq = em.op("V", lambda v: v.tensor_copy(Dd[:, CKCQ + 16:CKCQ + 32], pcol(5)))
            gS = guards.get("SCAN")
            w = [("pV", cpq)] + ([gS] if gS else [])
            em.op("T", lambda t: t.matmul(pSCAN[:, 0:16], cs["c_triu"][:],
                                          Dd[:, CKCQ:CKCQ + 16], start=True, stop=False),
                  waits=w, chain=False)
            em.op("T", lambda t: t.matmul(pSCAN[:, 0:16], cs["c_ones_row_bf"][:],
                                          Dd[0:1, SROW:SROW + 16], start=False, stop=False),
                  chain=False)
            em.op("T", lambda t: t.matmul(pSCAN[:, 16:32], cs["c_triu"][:],
                                          Dd[:, CKCQ + 16:CKCQ + 32], start=False, stop=False),
                  chain=False)
            m4 = em.op("T", lambda t: t.matmul(pSCAN[:, 16:32], cs["c_ones_row_bf"][:],
                                               Dd[0:1, SROW + 16:SROW + 32],
                                               start=False, stop=True), chain=False)
            scA = ("pT", m4)
            gsm = guards.get("SMmm")
            w = [("pV", cpq)] + ([gsm] if gsm else []) + ([gsr] if gsr else [])
            em.op("T", lambda t: t.matmul(pSM[0:1, 100:116], cs["c_ones_col"][:],
                                          Dd[:, CKCQ:CKCQ + 16],
                                          start=False, stop=False,
                                          skip_group_check=True),
                  waits=w, chain=False)
            em.op("T", lambda t: t.matmul(pSM[0:1, 116:132], cs["c_ones_col"][:],
                                          Dd[:, CKCQ + 16:CKCQ + 32],
                                          start=False, stop=False,
                                          skip_group_check=True), chain=False)

            ev1 = em.op("V", lambda v: v.tensor_copy(f32s[:, EV1:EV1 + 160],
                                                     pPACD[:, 0:160]),
                        waits=[phA])
            em.op("V", lambda v: v.tensor_add(
                f32s[:, FB + 32:FB + 48], ecol(0), ecol(4)))
            em.op("V", lambda v: v.tensor_add(
                f32s[:, FB + 48:FB + 64], ecol(1), ecol(7)))
            em.op("V", lambda v: v.tensor_add(
                f32s[:, FB + 64:FB + 80], ecol(5), pSCAN[:, 0:16]), waits=[scA])
            a2 = em.op("V", lambda v: v.tensor_add(
                f32s[:, FB + 80:FB + 96], ecol(8), pSCAN[:, 16:32]))
            guards["SCAN"] = ("pV", a2)
            em.op("V", lambda v: v.scalar_tensor_tensor(
                f32s[:, FB:FB + 32], f32s[:, FB + 64:FB + 96], EPS_B,
                f32s[:, FB + 32:FB + 64], ALU.mult, ALU.add))
            em.op("V", lambda v, rlp_col=rlp_col: v.tensor_scalar(
                f32s[:, FB:FB + 32], f32s[:, FB:FB + 32], 64.0 * EPS_B * EPS_B,
                rlp_col, ALU.add, ALU.mult))
            em.op("V", lambda v: v.reciprocal(f32s[:, FB:FB + 32], f32s[:, FB:FB + 32]))
            em.op("V", lambda v: v.tensor_copy(
                Dd[:, QVEC:QVEC + 32].rearrange("p (h c) -> p h c", c=2)[:, :, 1],
                f32s[:, FB:FB + 16]))
            c4 = em.op("V", lambda v: v.tensor_copy(
                Dd[:, KVEC:KVEC + 32].rearrange("p (h c) -> p h c", c=2)[:, :, 1],
                f32s[:, FB + 16:FB + 32]))
            sis_done = ("pV", c4)
            if DBG_SCAN == "phB":
                return

            mmC = None
            for h in range(16):
                w = [sis_done] if h == 0 else []
                mmC = em.op("T", lambda t, h=h:
                            t.matmul(pPACD[:, 10 * h + 2:10 * h + 3], stm(h),
                                     Dd[:, KVEC + 2 * h + 1:KVEC + 2 * h + 2],
                                     start=False, stop=False, skip_group_check=True),
                            waits=w, chain=False)
                mmC = em.op("T", lambda t, h=h:
                            t.matmul(pPACD[:, 10 * h + 3:10 * h + 4], stb(h),
                                     Dd[:, QVEC + 2 * h + 1:QVEC + 2 * h + 2],
                                     start=False, stop=False, skip_group_check=True),
                            chain=False)
            phC = ("pT", mmC)
            if DBG_SCAN == "phC":
                return

            em.op("V", lambda v: v.tensor_mul(
                Dd[:, CS2:CS2 + 16], Dd[:, CKCQ:CKCQ + 16],
                Dd[:, KVEC:KVEC + 32].rearrange("p (h c) -> p h c", c=2)[:, :, 1]))
            x2 = em.op("V", lambda v: v.tensor_mul(
                Dd[:, CS2 + 16:CS2 + 32], Dd[:, CKCQ + 16:CKCQ + 32],
                Dd[:, QVEC:QVEC + 32].rearrange("p (h c) -> p h c", c=2)[:, :, 1]))
            gS = guards.get("SCAN")
            w = [("pV", x2)] + ([gS] if gS else [])
            em.op("T", lambda t: t.matmul(pSCAN[:, 32:48], cs["c_triu"][:],
                                          Dd[:, CS2:CS2 + 16], start=True, stop=False),
                  waits=w, chain=False)
            em.op("T", lambda t: t.matmul(pSCAN[:, 32:48], cs["c_ones_row_bf"][:],
                                          Dd[0:1, SROW + 32:SROW + 48],
                                          start=False, stop=False), chain=False)
            em.op("T", lambda t: t.matmul(pSCAN[:, 48:64], cs["c_triu"][:],
                                          Dd[:, CS2 + 16:CS2 + 32], start=False, stop=False),
                  chain=False)
            n4 = em.op("T", lambda t: t.matmul(pSCAN[:, 48:64], cs["c_ones_row_bf"][:],
                                               Dd[0:1, SROW + 48:SROW + 64],
                                               start=False, stop=True), chain=False)
            scC = ("pT", n4)
            em.op("T", lambda t: t.matmul(pSM[0:1, 132:148], cs["c_ones_col"][:],
                                          Dd[:, CS2:CS2 + 16],
                                          start=False, stop=False, skip_group_check=True),
                  waits=[("pV", x2)], chain=False)
            em.op("T", lambda t: t.matmul(pSM[0:1, 148:164], cs["c_ones_col"][:],
                                          Dd[:, CS2 + 16:CS2 + 32],
                                          start=False, stop=False, skip_group_check=True),
                  chain=False)

            ev2 = em.op("V", lambda v: v.tensor_copy(
                f32s[:, EV2:EV2 + 32].rearrange("p (h c) -> p h c", c=2),
                pPACD[:, 0:160].rearrange("p (h c) -> p h c", c=10)[:, :, 2:4]),
                waits=[phC])
            guards["PACD"] = ("pV", ev2)
            b0 = em.op("V", lambda v: v.tensor_add(
                f32s[:, FC:FC + 16], ec2[:, :, 0], ecol(6)))
            b0b = em.op("V", lambda v: v.tensor_add(
                f32s[:, FC + 16:FC + 32], ec2[:, :, 1], ecol(9)))
            em.op("V", lambda v: v.tensor_add(
                f32s[:, FC + 96:FC + 112], ecol(5), pSCAN[:, 32:48]), waits=[scC])
            b2 = em.op("V", lambda v: v.tensor_add(
                f32s[:, FC + 112:FC + 128], ecol(8), pSCAN[:, 48:64]))
            guards["SCAN"] = ("pV", b2)
            em.op("V", lambda v: v.scalar_tensor_tensor(
                f32s[:, FC:FC + 32], f32s[:, FC + 96:FC + 128], EPS_B,
                f32s[:, FC:FC + 32], ALU.mult, ALU.add))
            em.op("V", lambda v, rlp_col=rlp_col: v.tensor_scalar(
                f32s[:, FC:FC + 32], f32s[:, FC:FC + 32], 64.0 * EPS_B * EPS_B,
                rlp_col, ALU.add, ALU.mult))
            b4 = em.op("V", lambda v: v.tensor_scalar(
                f32s[:, FC + 16:FC + 32], f32s[:, FC + 16:FC + 32], 1.0, -1.0,
                ALU.min, ALU.max))
            s1 = em.op("S", lambda s: s.activation(f32s[:, FC + 32:FC + 48],
                                                   f32s[:, FC:FC + 16], AF.Exp,
                                                   scale=-1.0), waits=[("pV", b4)])
            s2 = em.op("S", lambda s: s.activation(f32s[:, FC + 48:FC + 64],
                                                   f32s[:, FC + 16:FC + 32], AF.Exp))
            em.op("V", lambda v: v.tensor_scalar(
                f32s[:, FC + 32:FC + 48], f32s[:, FC + 32:FC + 48], 1.0, None, ALU.add),
                waits=[("pS", s1)])
            em.op("V", lambda v: v.reciprocal(f32s[:, FC + 32:FC + 48],
                                              f32s[:, FC + 32:FC + 48]))
            c5 = em.op("V", lambda v: v.tensor_copy(Dd[:, ECOL:ECOL + 16],
                                                    f32s[:, FC + 48:FC + 64]),
                       waits=[("pS", s2)])
            gS = guards.get("SCAN")
            w = [("pV", c5)] + ([gS] if gS else [])
            em.op("T", lambda t: t.matmul(pSCAN[:, 64:80], cs["c_triu"][:],
                                          Dd[:, ECOL:ECOL + 16], start=True, stop=False),
                  waits=w, chain=False)
            e2 = em.op("T", lambda t: t.matmul(pSCAN[:, 64:80], cs["c_ones_row_bf"][:],
                                               Dd[0:1, SROW + 64:SROW + 80],
                                               start=False, stop=True), chain=False)
            em.op("T", lambda t: t.matmul(pSM[0:1, 164:180], cs["c_ones_col"][:],
                                          Dd[:, ECOL:ECOL + 16],
                                          start=False, stop=False, skip_group_check=True),
                  waits=[("pV", c5)], chain=False)
            d1 = em.op("V", lambda v: v.reciprocal(f32s[:, FC + 64:FC + 80],
                                                   pSCAN[:, 64:80]),
                       waits=[("pT", e2)])
            guards["SCAN"] = ("pV", d1)
            d3 = em.op("V", lambda v, lp_col=lp_col: v.scalar_tensor_tensor(
                f32s[:, FC + 64:FC + 80], f32s[:, FC + 48:FC + 64], lp_col,
                f32s[:, FC + 64:FC + 80], ALU.mult, ALU.mult))
            d5 = em.op("V", lambda v, rlp_col=rlp_col: v.scalar_tensor_tensor(
                f32s[:, FC + 80:FC + 96], f32s[:, FB:FB + 16], rlp_col,
                f32s[:, FC + 32:FC + 48], ALU.mult, ALU.mult))
            comp_done = ("pV", d5)

            gvs = guards.get("VSb")
            vs_ev = None
            for h in range(16):
                w = []
                if h == 0:
                    w = [comp_done]
                    if gvs:
                        w.append(gvs)
                vs_ev = em.op("V", lambda v, h=h, lt=lt:
                              v.tensor_scalar(Dd[:, VS + 64 * h:VS + 64 * (h + 1)],
                                              Cc[:, lt * 1024 + 64 * h:lt * 1024 + 64 * (h + 1)],
                                              f32s[:, FC + 64 + h:FC + 65 + h], None,
                                              ALU.mult),
                              waits=w)
            vs_done = ("pV", vs_ev)
            if DBG_SCAN == "phD":
                return

            for wave in range(2):
                bank = pOUT if wave == 0 else pOUT2
                og = guards.get("OUTw" + str(wave))
                o2 = None
                for hh in range(8):
                    h = wave * 8 + hh
                    w = []
                    if hh == 0:
                        w = [vs_done]
                        if og:
                            w.append(og)
                    em.op("T", lambda t, h=h, hh=hh, bank=bank: t.matmul(
                        bank[:, 64 * hh:64 * hh + 64],
                        stm(h), Dd[:, VS + 64 * h:VS + 64 * (h + 1)],
                        start=(hh == 0), stop=False), waits=w, chain=False)
                    o2 = em.op("T", lambda t, h=h, hh=hh, lt=lt, bank=bank: t.matmul(
                        bank[:, 64 * hh:64 * hh + 64], pairt(h),
                        A[:, AKV + 64 * h:AKV + 64 * (h + 1)],
                        start=False, stop=(hh == 7)), chain=False)
                ev = None
                for hh in range(8):
                    h = wave * 8 + hh
                    ev = em.op("V", lambda v, h=h, hh=hh, lt=lt, bank=bank: v.tensor_scalar(
                        Dd[:, ATK + lt * 1024 + 64 * h:ATK + lt * 1024 + 64 * (h + 1)],
                        bank[:, 64 * hh:64 * hh + 64],
                        f32s[:, FC + 80 + h:FC + 81 + h], None, ALU.mult),
                        waits=([("pT", o2)] if hh == 0 else []))
                guards["OUTw" + str(wave)] = ("pV", ev)
            if DBG_SCAN == "out":
                return

            smm = None
            for h in range(16):
                p, h2 = h // 2, h % 2
                r0 = 64 * h2
                w = []
                if h == 0:
                    g = guards.get("SMrd")
                    if g:
                        w.append(g)
                    g = guards.get("KVrd")
                    if g:
                        w.append(g)
                    w.append(vs_done)
                smm = em.op("T", lambda t, h=h, p=p, r0=r0, h2=h2: t.matmul(
                    pKV[r0:r0 + 64, 64 * p:64 * (p + 1)],
                    Dd[:, KTOK + 128 * p + 64 * h2:KTOK + 128 * p + 64 * h2 + 64],
                    Dd[:, VS + 64 * h:VS + 64 * (h + 1)],
                    start=False, stop=False, skip_group_check=True),
                    waits=w, chain=False)
                smm = em.op("T", lambda t, h=h, p=p, r0=r0, h2=h2: t.matmul(
                    pSM[r0:r0 + 64, 2 * h:2 * h + 2],
                    Dd[:, KTOK + 128 * p + 64 * h2:KTOK + 128 * p + 64 * h2 + 64],
                    Dd[:, KVEC + 2 * h:KVEC + 2 * h + 2],
                    start=False, stop=False, skip_group_check=True),
                    chain=False)
                smm = em.op("T", lambda t, h=h, p=p, r0=r0, h2=h2: t.matmul(
                    pSM[r0:r0 + 64, 32 + 2 * h:32 + 2 * h + 2],
                    Dd[:, QTOK + 128 * p + 64 * h2:QTOK + 128 * p + 64 * h2 + 64],
                    Dd[:, QVEC + 2 * h:QVEC + 2 * h + 2],
                    start=False, stop=False, skip_group_check=True),
                    chain=False)
            st_done = ("pT", smm)
            guards["SMmm"] = st_done
            guards["VSb"] = st_done

            k1a = em.op("V", lambda v: v.tensor_copy(
                A[:, AKV:AKV + 1024].rearrange("p (h c) -> p h c", c=64)[0:64, 0::2, :],
                pKV[0:64, 0:512].rearrange("p (h c) -> p h c", c=64)),
                waits=[st_done])
            k1 = em.op("V", lambda v: v.tensor_copy(
                A[:, AKV:AKV + 1024].rearrange("p (h c) -> p h c", c=64)[64:128, 1::2, :],
                pKV[64:128, 0:512].rearrange("p (h c) -> p h c", c=64)))
            k2 = em.op("V", lambda v: v.tensor_copy(
                Dd[:, KST3:KST3 + 48].rearrange("p (h c) -> p h c", c=3)[:, :, 0::2],
                pSM[:, 0:32].rearrange("p (h c) -> p h c", c=2)))
            k3 = em.op("V", lambda v: v.tensor_copy(
                Dd[:, QST3:QST3 + 48].rearrange("p (h c) -> p h c", c=3)[:, :, 0::2],
                pSM[:, 32:64].rearrange("p (h c) -> p h c", c=2)))
            k4 = em.op("S", lambda s: s.copy(Dd[0:1, SROW:SROW + 80], pSM[0:1, 100:180]),
                       waits=[st_done])
            guards["SMrd"] = ("pS", k4)
            guards["KVrd"] = ("pV", k3)

        def wo_h_phase(half):
            gb_tok = emit_gb_rebuild("g1_row", "b1ln_row", STB,
                                     [("pT", em.last["T"]), ("pV", em.last["V"]),
                                      ("pS", em.last["S"])])
            guards["PACD"] = gb_tok
            guards["SCAN"] = gb_tok
            lt2 = ltT2 = 0
            for i in range(8):
                lt2, ltT2 = emit_transpose_1024(
                    lambda j, i=i: Dd[:, ATK + i * 1024 + 128 * j:ATK + i * 1024 + 128 * (j + 1)],
                    lambda j, i=i: Cc[:, UCM + j * 1024 + i * 128:UCM + j * 1024 + (i + 1) * 128],
                    "ptr", wait_src=([("pV", em.last["V"])] if i == 0 else []),
                    gdict=guards)
            atr = [("pV", lt2), ("pT", ltT2)]
            em.waits("G", atr + [gb_tok])
            bo_d = em.dma("G", Dd[0:1, STM:STM + 1024], D["bo_row"], ch="W")
            wo_d = em.dma("G", Bb[:, 0:8192].rearrange("p (k f) -> p k f", f=1024),
                          D["wo"].rearrange("(k p) f -> p k f", p=128), ch="W")
            for i in range(8):
                ii = half * 8 + i
                for fc in range(2):
                    bi = 4 + fc
                    g = guards.get(bi)
                    w0 = [wo_d, bo_d, gb_tok] + atr
                    if g:
                        w0.append(g)
                    mm = None
                    for k in range(NC):
                        mm = em.op("T", lambda t, k=k, i=i, fc=fc, bi=bi:
                                   t.matmul(pk[bi][:, 0:512],
                                            Cc[:, UCM + k * 1024 + i * 128:UCM + k * 1024 + (i + 1) * 128],
                                            Bb[:, k * 1024 + fc * 512:k * 1024 + (fc + 1) * 512],
                                            start=(k == 0), stop=False),
                                   waits=(w0 if k == 0 else []), chain=False)
                    mm = em.op("T", lambda t, fc=fc, bi=bi:
                               t.matmul(pk[bi][:, 0:512], cs["c_ones_row_bf"][:],
                                        Dd[0:1, STM + fc * 512:STM + (fc + 1) * 512],
                                        start=False, stop=True), chain=False)
                    su = em.op("S", lambda s, ii=ii, fc=fc:
                               s.activation(sc32[:, fc * 512:(fc + 1) * 512],
                                            x_sb[:, ii * 1024 + fc * 512:ii * 1024 + (fc + 1) * 512],
                                            AF.Identity,
                                            bias=f32s[:, 100 + 16 * ii:100 + 16 * ii + 1],
                                            scale=f32s[:, 100 + 16 * ii + 1:100 + 16 * ii + 2]),
                               waits=[("pT", mm)])
                    av = em.op("V", lambda v, fc=fc, bi=bi:
                               v.tensor_add(sc32[:, fc * 512:(fc + 1) * 512],
                                            sc32[:, fc * 512:(fc + 1) * 512],
                                            pk[bi][:, 0:512]),
                               waits=[("pS", su)])
                    guards[bi] = ("pV", av)
                em.op("V", lambda v: v.bn_stats(f32s[:, 2:8], sc32[:, 0:512]))
                em.op("V", lambda v: v.bn_stats(f32s[:, 8:14], sc32[:, 512:1024]))
                t3 = em.op("V", lambda v: v.bn_aggr(f32s[:, 14:16], f32s[:, 2:14]))
                em.op("V", lambda v: v.tensor_scalar(f32s[:, 1:2], f32s[:, 15:16],
                                                     EPS_LN, None, ALU.add))
                t4 = em.op("S", lambda s: s.activation(f32s[:, 1:2], f32s[:, 1:2], AF.Sqrt),
                           waits=[("pV", em.last["V"])])
                em.op("V", lambda v: v.reciprocal(f32s[:, 1:2], f32s[:, 1:2]),
                      waits=[("pS", t4)])
                t6 = em.op("V", lambda v: v.tensor_scalar(f32s[:, 0:1], f32s[:, 14:15],
                                                          f32s[:, 1:2], -1.0,
                                                          ALU.mult, ALU.mult))
                t7 = em.op("S", lambda s, i=i: s.activation(
                    Dd[:, ATK + i * 1024:ATK + (i + 1) * 1024], sc32[:, 0:1024],
                    AF.Identity, bias=f32s[:, 0:1], scale=f32s[:, 1:2]),
                    waits=[("pV", t6)])
                em.op("V", lambda v, i=i: v.tensor_mul(
                    Dd[:, ATK + i * 1024:ATK + (i + 1) * 1024],
                    Dd[:, ATK + i * 1024:ATK + (i + 1) * 1024], g_bc[:]),
                    waits=[("pS", t7)])
                hb = em.op("V", lambda v, i=i: v.tensor_add(
                    Dd[:, ATK + i * 1024:ATK + (i + 1) * 1024],
                    Dd[:, ATK + i * 1024:ATK + (i + 1) * 1024], b_bc[:]))
                lt2, ltT2 = emit_transpose_1024(
                    lambda j, i=i: Dd[:, ATK + i * 1024 + 128 * j:ATK + i * 1024 + 128 * (j + 1)],
                    lambda j, i=i, ii=ii: A[:, j * STRIDE + 1 + ii * 128:j * STRIDE + 1 + (ii + 1) * 128],
                    "ptr", wait_src=[("pV", hb)], gdict=guards)
            return [("pV", lt2), ("pT", ltT2)]

        def ffn_ln2(half, h_done):
            gb_tok = emit_gb_rebuild("g2_row", "b2ln_row", 4096,
                                     [("pT", em.last["T"]), ("pV", em.last["V"]),
                                      ("pS", em.last["S"])] + h_done)
            guards["PACD"] = gb_tok
            guards["SCAN"] = gb_tok
            gelu_ev = None
            for fb in range(32):
                slot = fb % 2
                g = guards.get(("w1", slot))
                if g:
                    em.waits("G", [g])
                dt_ = em.dma("G", Dd[:, slot * 1024:(slot + 1) * 1024]
                             .rearrange("p (k f) -> p k f", f=128),
                             D["w1"].rearrange("(k p) f -> p k f", p=128)[:, :, 128 * fb:128 * (fb + 1)],
                             ch=("A" if slot == 0 else "B"))
                last_mm = None
                for tc in range(2):
                    bi = 4 + (fb * 2 + tc) % 2
                    gb = guards.get(bi)
                    w0 = [dt_] + h_done + [gb_tok]
                    if gb:
                        w0.append(gb)
                    mm = None
                    for k in range(NC):
                        mm = em.op("T", lambda t, k=k, tc=tc, bi=bi, slot=slot, half=half:
                                   t.matmul(pk[bi][:, 0:512],
                                            Dd[:, slot * 1024 + k * 128:slot * 1024 + (k + 1) * 128],
                                            A[:, k * STRIDE + 1 + half * 1024 + tc * 512:
                                              k * STRIDE + 1 + half * 1024 + (tc + 1) * 512],
                                            start=(k == 0), stop=(k == NC - 1)),
                                   waits=(w0 if k == 0 else []), chain=False)
                    buf = Cc if fb < 16 else Bb
                    gelu_ev = em.op("S", lambda s, fb=fb, tc=tc, bi=bi, buf=buf:
                                    s.activation(buf[:, (fb % 16) * 1024 + tc * 512:(fb % 16) * 1024 + (tc + 1) * 512],
                                                 pk[bi][:, 0:512], AF.Gelu,
                                                 bias=b1_c[:, fb:fb + 1], scale=1.0),
                                    waits=[("pT", mm)])
                    guards[bi] = ("pS", gelu_ev)
                    last_mm = mm
                guards[("w1", slot)] = ("pT", last_mm)
            g_done = ("pS", gelu_ev)

            em.waits("G", [("pT", em.last["T"])])
            for db in range(NC):
                slot = db % 2
                g = guards.get(("w2", slot))
                if g:
                    em.waits("G", [g])
                dt_ = em.dma("G", Dd[:, slot * 4096:(slot + 1) * 4096]
                             .rearrange("p (k f) -> p k f", f=128),
                             D["w2"].rearrange("(k p) f -> p k f", p=128)[:, :, 128 * db:128 * (db + 1)],
                             ch=("A" if slot == 0 else "B"))
                last_mm = None
                for tc in range(2):
                    bi = 4 + (db * 2 + tc) % 2
                    gb = guards.get(bi)
                    w0 = [dt_, g_done]
                    if gb:
                        w0.append(gb)
                    mm = None
                    for ft in range(32):
                        buf = Cc if ft < 16 else Bb
                        mm = em.op("T", lambda t, ft=ft, tc=tc, bi=bi, slot=slot, buf=buf:
                                   t.matmul(pk[bi][:, 0:512],
                                            Dd[:, slot * 4096 + ft * 128:slot * 4096 + (ft + 1) * 128],
                                            buf[:, (ft % 16) * 1024 + tc * 512:(ft % 16) * 1024 + (tc + 1) * 512],
                                            start=(ft == 0), stop=(ft == 31)),
                                   waits=(w0 if ft == 0 else []), chain=False)
                    sv = em.op("S", lambda s, db=db, tc=tc, bi=bi:
                               s.activation(Dd[:, 8192 + db * 1024 + tc * 512:8192 + db * 1024 + (tc + 1) * 512],
                                            pk[bi][:, 0:512], AF.Identity,
                                            bias=pcols["b2_col"][:, db:db + 1], scale=1.0),
                               waits=[("pT", mm)])
                    av = em.op("V", lambda v, db=db, tc=tc, half=half:
                               v.tensor_add(A[:, db * STRIDE + 1 + half * 1024 + tc * 512:
                                              db * STRIDE + 1 + half * 1024 + (tc + 1) * 512],
                                            A[:, db * STRIDE + 1 + half * 1024 + tc * 512:
                                              db * STRIDE + 1 + half * 1024 + (tc + 1) * 512],
                                            Dd[:, 8192 + db * 1024 + tc * 512:8192 + db * 1024 + (tc + 1) * 512]),
                               waits=[("pS", sv)])
                    guards[bi] = ("pV", av)
                    last_mm = mm
                guards[("w2", slot)] = ("pT", last_mm)
            s_done = [("pV", em.last["V"])]

            for i in range(8):
                ii = half * 8 + i
                lt2, ltT2 = emit_transpose_1024(
                    lambda j, i=i, half=half: A[:, j * STRIDE + 1 + half * 1024 + i * 128:
                                                j * STRIDE + 1 + half * 1024 + (i + 1) * 128],
                    lambda j, i=i: Dd[:, 8192 + i * 1024 + 128 * j:8192 + i * 1024 + 128 * (j + 1)],
                    "ptr", wait_src=(s_done if i == 0 else []), gdict=guards)
                stok = Dd[:, 8192 + i * 1024:8192 + (i + 1) * 1024]
                em.op("V", lambda v, stok=stok: v.bn_stats(f32s[:, 2:8], stok[:, 0:512]),
                      waits=[("pV", lt2)])
                em.op("V", lambda v, stok=stok: v.bn_stats(f32s[:, 8:14], stok[:, 512:1024]))
                em.op("V", lambda v: v.bn_aggr(f32s[:, 14:16], f32s[:, 2:14]))
                em.op("V", lambda v: v.tensor_scalar(f32s[:, 1:2], f32s[:, 15:16],
                                                     EPS_LN, None, ALU.add))
                t4 = em.op("S", lambda s: s.activation(f32s[:, 1:2], f32s[:, 1:2], AF.Sqrt),
                           waits=[("pV", em.last["V"])])
                em.op("V", lambda v: v.reciprocal(f32s[:, 1:2], f32s[:, 1:2]),
                      waits=[("pS", t4)])
                t6 = em.op("V", lambda v: v.tensor_scalar(f32s[:, 0:1], f32s[:, 14:15],
                                                          f32s[:, 1:2], -1.0,
                                                          ALU.mult, ALU.mult))
                t7 = em.op("S", lambda s, stok=stok: s.activation(
                    stok, stok, AF.Identity, bias=f32s[:, 0:1], scale=f32s[:, 1:2]),
                    waits=[("pV", t6)])
                em.op("V", lambda v, stok=stok: v.tensor_mul(sc32[:, 0:1024], stok, g_bc[:]),
                      waits=[("pS", t7)])
                em.op("V", lambda v: v.tensor_add(sc32[:, 0:1024], sc32[:, 0:1024], b_bc[:]))
                em.op("V", lambda v, ii=ii: v.tensor_add(
                    x_sb[:, ii * 1024:(ii + 1) * 1024],
                    x_sb[:, ii * 1024:(ii + 1) * 1024], sc32[:, 0:1024]))

        h_done = None
        for half in range(2):
            u_done = emit_ln_half(half)
            if half == 1:
                u_done = u_done + [("pT", em.last["T"]), ("pV", em.last["V"])]
            if DBG_STOP == "ln":
                em.barrier(); return
            qd = proj_qk("q", 0, pcols["bq_col"], u_done)
            kd = proj_qk("k", 8192, pcols["bk_col"], u_done)
            vd = proj_v(u_done)
            qkv_done = [qd, kd, vd]
            if DBG_STOP == "proj":
                em.barrier(); return
            for lt in range(8):
                scan_chunk(half * 8 + lt, qkv_done if lt == 0 else [])
                if DBG_STOP == "chunk0" and lt == 0:
                    em.barrier(); return
            if DBG_STOP == "scan":
                em.barrier(); return
            h_done = wo_h_phase(half)
            if DBG_STOP == "wo":
                em.barrier(); return
        for half in range(2):
            ffn_ln2(half, h_done)
        em.barrier()

    for k in CONST_SPECS:
        em.dma("Y", cs[k][:], D[k])
    if not skip_params:
        for nm in PARAM_COLS:
            em.dma("Y", pcols[nm][:], D[nm])
        em.dma("Y", b1_c[:], D["b1_col"])
    em.barrier()

    for b in range(B_PER_CORE):
        src_b = x_in.rearrange("b (i p) c -> b i p c", p=128)[b]
        xload_toks.clear()
        for i in range(NT):
            xload_toks[i] = em.dma("Y", x_sb[:, i * 1024:(i + 1) * 1024], src_b[i])
        subs_b = subs[b] if isinstance(subs, dict) else subs
        if not subs_b or subs_b[0] != 1:
            em.barrier()
            xload_toks.clear()
        for s_id in subs_b:
            if s_id == 1:
                emit_attn_a()
            if s_id == 3:
                emit_attn_b()
            if s_id in (2, 4):
                emit_ffn()
        em.dma("Y", out.rearrange("b (i p) c -> b p i c", p=128)[b],
               x_sb[:].rearrange("p (i c) -> p i c", c=1024))
        em.barrier()

    em.waits("Y", em.snap())

    with nc.Block() as blk:
        em.build(blk)
    rem = nc.sbuf_bytes_remaining
    st.close()
    return nc, rem


def make_inputs_per_core(inputs, core):
    x_full = np.asarray(inputs["L0_attn_a_query"], np.float32)
    W = {
        "wipT": np.ascontiguousarray(np.asarray(inputs["L0_attn_a_in_proj_weight"], np.float32).T),
        "woaT": np.ascontiguousarray(np.asarray(inputs["L0_attn_a_out_proj_weight"], np.float32).T),
        "wrT": np.ascontiguousarray(np.asarray(inputs["L0_ffn_W_r"], np.float32).T),
        "wkT": np.ascontiguousarray(np.asarray(inputs["L0_ffn_W_k"], np.float32).T),
        "wvT": np.ascontiguousarray(np.asarray(inputs["L0_ffn_W_v"], np.float32).T),
        "wq": np.ascontiguousarray(np.asarray(inputs["L0_attn_b_Wq"], np.float32)),
        "wk": np.ascontiguousarray(np.asarray(inputs["L0_attn_b_Wk"], np.float32)),
        "wv": np.ascontiguousarray(np.asarray(inputs["L0_attn_b_Wv"], np.float32)),
        "wo": np.ascontiguousarray(np.asarray(inputs["L0_attn_b_Wo"], np.float32)),
        "w1": np.ascontiguousarray(np.asarray(inputs["L0_attn_b_W1"], np.float32)),
        "w2": np.ascontiguousarray(np.asarray(inputs["L0_attn_b_W2"], np.float32)),
    }
    for nm, key in [("mur_col", "L0_ffn_mu_r"), ("muk_col", "L0_ffn_mu_k"),
                    ("bq_col", "L0_attn_b_bq"), ("bk_col", "L0_attn_b_bk"),
                    ("bv_col", "L0_attn_b_bv"), ("bo_col", "L0_attn_b_bo"),
                    ("b2_col", "L0_attn_b_b2")]:
        W[nm] = np.ascontiguousarray(np.asarray(inputs[key], np.float32).reshape(8, 128).T)
    W["b1_col"] = np.ascontiguousarray(
        np.asarray(inputs["L0_attn_b_b1"], np.float32).reshape(32, 128).T)
    W["bv_row"] = np.asarray(inputs["L0_attn_b_bv"], np.float32).reshape(1, C)
    W["bo_row"] = np.asarray(inputs["L0_attn_b_bo"], np.float32).reshape(1, C)
    W["g1_row"] = np.asarray(inputs["L0_attn_b_ln1_g"], np.float32).reshape(1, C)
    W["b1ln_row"] = np.asarray(inputs["L0_attn_b_ln1_b"], np.float32).reshape(1, C)
    W["g2_row"] = np.asarray(inputs["L0_attn_b_ln2_g"], np.float32).reshape(1, C)
    W["b2ln_row"] = np.asarray(inputs["L0_attn_b_ln2_b"], np.float32).reshape(1, C)
    m = {"x": np.ascontiguousarray(x_full[2 * core:2 * core + 2])}
    m.update(W)
    m.update(_consts_host())
    return m


def kernel(**inputs):
    subs = inputs.pop("_subs", (1, 2, 3, 4))
    nc, rem = build_program(subs=subs)
    in_maps = [make_inputs_per_core(inputs, core) for core in range(8)]
    res = run_bass_kernel_spmd(nc, in_maps, core_ids=list(range(8)))
    outs = [res.results[i]["out"] for i in range(8)]
    return np.concatenate(outs, axis=0).astype(np.float32)


# version 17
# speedup vs baseline: 1.0062x; 1.0062x over previous
import contextlib
import numpy as np
import ml_dtypes
import concourse.bass as bass
import concourse.mybir as mybir
from concourse.bass_utils import run_bass_kernel_spmd

F32 = mybir.dt.float32
BF16 = mybir.dt.bfloat16
AF = mybir.ActivationFunctionType
ALU = mybir.AluOpType
AX = mybir.AxisListType

B_PER_CORE = 2
T = 2048
C = 1024
NT = T // 128
NC = C // 128
EPS_LN = 1e-5
EPS_B = 1e-6
H = 16
DH = 64
F_FFN = 4096
STRIDE = 2049
DBG_STOP = None
DBG_SCAN = None
USE_MM_TRANSPOSE = True
SKIP_EVAC = False


class Em:
    ENGS = ("T", "V", "S", "G", "Y")

    def __init__(self, nc, st):
        self.nc = nc
        self.ops = {e: [] for e in self.ENGS}
        self.sems = {}
        self.cnt = {}
        self.st = st
        for name in ("pT", "pV", "pS", "pG", "dY", "dG", "dS", "dGA", "dGB", "dGW"):
            self.sems[name] = st.enter_context(nc.semaphore(name))
            self.cnt[name] = 0
        self.last = {e: 0 for e in self.ENGS}

    def _wait_list(self, eng_obj, waits):
        best = {}
        for s, v in waits:
            if not v:
                continue
            assert v <= self.cnt[s], f"wait on future {s}={v} > {self.cnt[s]}"
            best[s] = max(best.get(s, 0), v)
        for s, v in best.items():
            eng_obj.wait_ge(self.sems[s], v)

    def op(self, eng, fn, waits=(), chain=None):
        if chain is None:
            chain = eng in ("V", "S", "G")
        prog = "p" + eng
        waits = list(waits)
        if chain and self.last[eng]:
            waits.append((prog, self.last[eng]))
        self.cnt[prog] += 1
        tok = self.cnt[prog]
        self.last[eng] = tok
        sem = self.sems[prog]
        me = self

        def run(eng_obj, fn=fn, waits=tuple(waits), sem=sem):
            me._wait_list(eng_obj, waits)
            inst = fn(eng_obj)
            assert inst is not None, "compute op must return its instruction"
            inst.then_inc(sem, 1)
        self.ops[eng].append(run)
        return tok

    def waits(self, eng, waits):
        me = self
        waits = tuple(waits)

        def run(eng_obj, waits=waits):
            me._wait_list(eng_obj, waits)
        self.ops[eng].append(run)

    def dma(self, eng, dst, src, ch=None):
        if ch is not None:
            sem = "dG" + ch
        else:
            sem = "dG" if eng == "G" else ("dS" if eng == "S" else "dY")
        self.cnt[sem] += 16
        tok = self.cnt[sem]
        s = self.sems[sem]
        me = self

        def run(eng_obj, dst=dst, src=src, s=s):
            eng_obj.dma_start(dst, src).then_inc(s, 16)
        self.ops[eng].append(run)
        return (sem, tok)

    def dma_waits(self, eng, waits):
        self.waits(eng, waits)

    def snap(self):
        return [(s, self.cnt[s]) for s in ("pT", "pV", "pS", "pG", "dY", "dG",
                                           "dS", "dGA", "dGB", "dGW")]

    def barrier(self, exclude=()):
        sn = [w for w in self.snap() if w[0] not in exclude]
        for e in ("T", "V", "S", "G", "Y"):
            self.waits(e, [w for w in sn if w[0] != "p" + e])

    def build(self, blk):
        for eng, attr in (("Y", "sync"), ("G", "gpsimd"), ("T", "tensor"),
                          ("S", "scalar"), ("V", "vector")):
            ops = self.ops[eng]

            def make(ops=ops):
                def body(eng_obj):
                    for f in ops:
                        f(eng_obj)
                return body
            getattr(blk, attr)(make())


def _bf(x):
    return np.ascontiguousarray(x, dtype=ml_dtypes.bfloat16)


def _consts_host():
    ident = np.eye(128, dtype=ml_dtypes.bfloat16)
    tri = np.triu(np.ones((128, 128), np.float32))
    triu4 = _bf(np.tile(tri, (1, 4)))
    triu_f = _bf(tri)
    jj = np.arange(128)[:, None]
    uu = np.arange(896)[None, :]
    dband = _bf((jj <= uu - 384).astype(np.float32))
    ones_row = np.ones((1, 128), np.float32)
    ones_row_bf = np.ones((1, 128), dtype=ml_dtypes.bfloat16)
    ones_col_bf = np.ones((128, 1), dtype=ml_dtypes.bfloat16)
    lpos = (np.arange(128)[:, None] + 1 + 128 * np.arange(NT)[None, :]).astype(np.float32)
    rlpos = (1.0 / lpos).astype(np.float32)
    half64 = np.full((64, 1), 0.5, np.float32)
    half64_hi = np.full((128, 1), 0.5, np.float32)
    return dict(c_ident=ident, c_triu4=triu4, c_triu=triu_f, c_dband=dband,
                c_ones_row=ones_row, c_ones_row_bf=ones_row_bf,
                c_ones_col=ones_col_bf, c_lpos=lpos, c_rlpos=rlpos,
                c_half64=half64, c_half64_hi=half64_hi)


CONST_SPECS = dict(
    c_ident=([128, 128], BF16), c_triu4=([128, 512], BF16), c_triu=([128, 128], BF16),
    c_dband=([128, 896], BF16), c_ones_row=([1, 128], F32),
    c_ones_row_bf=([1, 128], BF16), c_ones_col=([128, 1], BF16),
    c_lpos=([128, NT], F32), c_rlpos=([128, NT], F32),
    c_half64=([64, 1], F32), c_half64_hi=([128, 1], F32),
)

PARAM_COLS = ["mur_col", "muk_col", "bq_col", "bk_col", "bv_col", "bo_col", "b2_col"]


def build_program(subs=(1, 2, 3, 4), ffn_stage=5, nt_lim=NT, skip_gbc=False, skip_params=False):
    nc = bass.Bass("TRN2", target_bir_lowering=False)
    st = contextlib.ExitStack()
    D = {}

    def din(name, shape, dt=F32):
        D[name] = nc.dram_tensor(name, shape, dt, kind="ExternalInput").ap()
        return D[name]

    x_in = din("x", [B_PER_CORE, T, C])
    out = nc.dram_tensor("out", [B_PER_CORE, T, C], F32, kind="ExternalOutput").ap()

    din("wipT", [C, 3 * C]); din("woaT", [C, C])
    din("wrT", [C, C]); din("wkT", [C, C]); din("wvT", [C, C])
    din("wq", [C, C]); din("wk", [C, C]); din("wv", [C, C]); din("wo", [C, C])
    din("w1", [C, F_FFN]); din("w2", [F_FFN, C])
    for nm in PARAM_COLS:
        din(nm, [128, 8])
    din("b1_col", [128, 32])
    din("bv_row", [1, C]); din("bo_row", [1, C])
    din("g1_row", [1, C]); din("b1ln_row", [1, C])
    din("g2_row", [1, C]); din("b2ln_row", [1, C])
    for k, (shape, dt) in CONST_SPECS.items():
        din(k, shape, dt)

    em = Em(nc, st)

    def sb(name, p, f, dt=BF16):
        return st.enter_context(nc.sbuf_tensor(name, [p, f], dt))

    def psum(name, p, f, dt=F32):
        return st.enter_context(nc.psum_tensor(name, [p, f], dt))

    x_sb = sb("x_sb", 128, NT * 1024, F32)
    A = sb("bufA", 128, NC * STRIDE)
    Bb = sb("bufB", 128, 16384)
    Cc = sb("bufC", 128, 16384)
    Dd = sb("bufD", 128, 16384)
    cs = {k: sb(k + "_s", sh[0], sh[1], dt) for k, (sh, dt) in CONST_SPECS.items()}
    f32s = sb("f32s", 128, 800, F32)
    pcols = {nm: sb(nm + "_s", 128, 8, F32) for nm in PARAM_COLS}
    b1_c = sb("b1_col_s", 128, 32, F32)
    g_bc = sb("g_bc", 128, 1024); b_bc = sb("b_bc", 128, 1024)
    sc32 = sb("sc32", 128, 1024, F32)

    eps_ln_c = f32s[:, 32:33]
    eps_b_c = f32s[:, 33:34]
    em.op("V", lambda v: v.memset(f32s[:, 32:33], EPS_LN))
    em.op("V", lambda v: v.memset(f32s[:, 33:34], EPS_B))

    pk = [psum(f"pk{i}", 128, 512) for i in range(6)]
    pk6 = psum("pk6", 128, 512)
    pk7 = psum("pk7", 128, 512)
    wpf = {}
    xload_toks = {}
    pc0 = pk6
    ptr_guard = {}
    negmargin_c = f32s[:, 34 + 64:34 + 65] if False else f32s[:, 798:799]
    em.op("V", lambda v: v.memset(f32s[:, 798:799], -14.0))

    ln_slot_guard = {}

    def emit_ln_tile(xt_ap, u_out_ap, stat_off, extra_wait=(), apply_wait=()):
        bns = f32s[:, stat_off + 2:stat_off + 14]
        agg = f32s[:, stat_off + 14:stat_off + 16]
        mcol = f32s[:, stat_off:stat_off + 1]
        rcol = f32s[:, stat_off + 1:stat_off + 2]
        g = ln_slot_guard.get(stat_off)
        em.op("V", lambda v: v.bn_stats(bns[:, 0:6], xt_ap[:, 0:512]),
              waits=list(extra_wait) + ([("pS", g)] if g else []))
        em.op("V", lambda v: v.bn_stats(bns[:, 6:12], xt_ap[:, 512:1024]))
        t3 = em.op("V", lambda v: v.bn_aggr(agg, bns))
        t4 = em.op("S", lambda s: s.activation(rcol, agg[:, 1:2], AF.Sqrt,
                                               bias=eps_ln_c, scale=1.0),
                   waits=[("pV", t3)])
        t5 = em.op("V", lambda v: v.reciprocal(rcol, rcol), waits=[("pS", t4)])
        t6 = em.op("V", lambda v: v.tensor_scalar(mcol, agg[:, 0:1], rcol, -1.0,
                                                  ALU.mult, ALU.mult))
        t7 = em.op("S", lambda s: s.activation(u_out_ap, xt_ap, AF.Identity,
                                               bias=mcol, scale=rcol),
                   waits=[("pV", t6)] + list(apply_wait))
        ln_slot_guard[stat_off] = t7
        return t7, mcol, rcol

    def emit_transpose_1024(src_fn, dst_fn, bank_name, wait_src, gdict=None):
        bpair = (4, 5) if bank_name == "ptr" else (2, 3)
        last_T = 0
        for r in range(2):
            bank = pk[bpair[r]]
            toks = []
            for j in range(4):
                jj = r * 4 + j
                w = list(wait_src if jj == 0 else [])
                if gdict is None:
                    g = ptr_guard.get((bank_name, r), 0)
                    if j == 0 and g:
                        w.append(("pV", g))
                else:
                    g = gdict.get(bpair[r])
                    if j == 0 and g:
                        w.append(g)
                toks.append(em.op("T", lambda t, jj=jj, j=j, bank=bank: t.matmul(
                    bank[:, 128 * j:128 * (j + 1)], src_fn(jj), cs["c_ident"][:],
                    start=True, stop=True),
                    waits=w, chain=False))
            last_T = toks[-1]
            for j in range(4):
                jj = r * 4 + j
                ev = em.op("V", lambda v, jj=jj, j=j, bank=bank: v.tensor_copy(
                    dst_fn(jj), bank[:, 128 * j:128 * (j + 1)]),
                    waits=[("pT", toks[-1])] if j == 0 else [])
            if gdict is None:
                ptr_guard[(bank_name, r)] = ev
            else:
                gdict[bpair[r]] = ("pV", ev)
        return ev, last_T


    ln_slot_guard = {}

    def emit_ln_tile(xt_ap, u_out_ap, stat_off, extra_wait=(), apply_wait=()):
        bns = f32s[:, stat_off + 2:stat_off + 14]
        agg = f32s[:, stat_off + 14:stat_off + 16]
        mcol = f32s[:, stat_off:stat_off + 1]
        rcol = f32s[:, stat_off + 1:stat_off + 2]
        g = ln_slot_guard.get(stat_off)
        em.op("V", lambda v: v.bn_stats(bns[:, 0:6], xt_ap[:, 0:512]),
              waits=list(extra_wait) + ([("pS", g)] if g else []))
        em.op("V", lambda v: v.bn_stats(bns[:, 6:12], xt_ap[:, 512:1024]))
        t3 = em.op("V", lambda v: v.bn_aggr(agg, bns))
        t4 = em.op("S", lambda s: s.activation(rcol, agg[:, 1:2], AF.Sqrt,
                                               bias=eps_ln_c, scale=1.0),
                   waits=[("pV", t3)])
        t5 = em.op("V", lambda v: v.reciprocal(rcol, rcol), waits=[("pS", t4)])
        t6 = em.op("V", lambda v: v.tensor_scalar(mcol, agg[:, 0:1], rcol, -1.0,
                                                  ALU.mult, ALU.mult))
        t7 = em.op("S", lambda s: s.activation(u_out_ap, xt_ap, AF.Identity,
                                               bias=mcol, scale=rcol),
                   waits=[("pV", t6)] + list(apply_wait))
        ln_slot_guard[stat_off] = t7
        return t7, mcol, rcol


    def emit_ln_pass_batched(tiles, dst_fn, scratch_off, wait_extra=(), gdict=None,
                             tile_waits=None):
        ag_last = None
        first = True
        for ii in tiles:
            so = 100 + 16 * ii
            g = ln_slot_guard.get(so)
            w = (list(wait_extra) if first else []) + ([("pS", g)] if g else [])
            if tile_waits is not None:
                w = w + list(tile_waits(ii))
            first = False
            em.op("V", lambda v, ii=ii, so=so: v.bn_stats(
                f32s[:, so + 2:so + 8], x_sb[:, ii * 1024:ii * 1024 + 512]), waits=w)
            em.op("V", lambda v, ii=ii, so=so: v.bn_stats(
                f32s[:, so + 8:so + 14], x_sb[:, ii * 1024 + 512:(ii + 1) * 1024]))
            ag_last = em.op("V", lambda v, so=so: v.bn_aggr(
                f32s[:, so + 14:so + 16], f32s[:, so + 2:so + 14]))
        n = len(tiles)
        i0 = tiles[0]
        slots = f32s[:, 100 + 16 * i0:100 + 16 * (i0 + n)].rearrange(
            "p (t c) -> p t c", c=16)
        gv = em.op("V", lambda v, slots=slots, n=n: v.tensor_scalar(
            f32s[:, 360:360 + n], slots[:, :, 15], EPS_LN, None, ALU.add))
        sq = em.op("S", lambda s, n=n: s.activation(f32s[:, 360:360 + n],
                                                    f32s[:, 360:360 + n], AF.Sqrt),
                   waits=[("pV", gv)])
        em.op("V", lambda v, n=n: v.reciprocal(f32s[:, 360:360 + n],
                                               f32s[:, 360:360 + n]),
              waits=[("pS", sq)])
        em.op("V", lambda v, slots=slots: v.tensor_copy(
            slots[:, :, 1], f32s[:, 360:360 + n]))
        em.op("V", lambda v, slots=slots: v.tensor_mul(
            f32s[:, 376:376 + n], slots[:, :, 14], f32s[:, 360:360 + n]))
        stats_done = em.op("V", lambda v, slots=slots: v.tensor_scalar(
            slots[:, :, 0], f32s[:, 376:376 + n], -1.0, None, ALU.mult))
        lt_v = lt_T = 0
        for i, ii in enumerate(tiles):
            so = 100 + 16 * ii
            ew = [("pT", lt_T)] if lt_T else []
            t7 = em.op("S", lambda s, ii=ii, so=so: s.activation(
                Cc[:, scratch_off:scratch_off + 1024],
                x_sb[:, ii * 1024:(ii + 1) * 1024], AF.Identity,
                bias=f32s[:, so:so + 1], scale=f32s[:, so + 1:so + 2]),
                waits=[("pV", stats_done)] + ew)
            ln_slot_guard[so] = t7
            lt_v, lt_T = emit_transpose_1024(
                lambda j: Cc[:, scratch_off + 128 * j:scratch_off + 128 * (j + 1)],
                lambda j, i=i, ii=ii: dst_fn(ii, j),
                "ptr", wait_src=[("pS", t7)], gdict=gdict)
        return lt_v, lt_T

    def emit_ln_pass(scratch_off, wait_extra=()):
        lt = None
        last_tr_T = 0
        for i in range(NT):
            ew = [("pT", last_tr_T)] if last_tr_T else []
            t_u, _, _ = emit_ln_tile(x_sb[:, i * 1024:(i + 1) * 1024],
                                     Cc[:, scratch_off:scratch_off + 1024],
                                     stat_off=16 * (i % 2),
                                     extra_wait=list(wait_extra) if i == 0 else (),
                                     apply_wait=ew)
            lt, last_tr_T = emit_transpose_1024(
                lambda j: Cc[:, scratch_off + 128 * j:scratch_off + 128 * (j + 1)],
                lambda j, i=i: A[:, j * STRIDE + 1 + i * 128:j * STRIDE + 1 + (i + 1) * 128],
                "ptr", wait_src=[("pS", t_u)])
        return lt, last_tr_T

    def pkx(i):
        return pk6 if i == 6 else (pk7 if i == 7 else pk[i])

    def emit_attn_a():
        QV, KV_, VV = 0, 2048, 4096
        PBUF = 6144
        SCR = 12288
        VT, WIPO, PT = 0, 4160, 10304

        lt, ltT = emit_ln_pass_batched(
            list(range(NT)),
            lambda ii, j: A[:, j * STRIDE + 1 + ii * 128:j * STRIDE + 1 + (ii + 1) * 128],
            SCR,
            tile_waits=(lambda ii: [xload_toks[ii]]) if xload_toks else None)
        ln_done = [("pV", lt), ("pT", ltT)]

        ones_tok = em.op("V", lambda v: v.memset(Dd[:, 0:4160], 1.0))

        wipT_r = D["wipT"].rearrange("(k p) f -> p k f", p=128)

        def dma_wip(hp, slot):
            g = wip_guard.get(slot)
            if g:
                em.waits("G", [("pT", g)])
            toks = []
            for o in range(3):
                col0 = 1024 * o + 128 * hp
                dst = Dd[:, WIPO + slot * 3072 + o * 1024:WIPO + slot * 3072 + (o + 1) * 1024]
                dst = dst.rearrange("p (k f) -> p k f", f=128)
                toks.append(em.dma("G", dst, wipT_r[:, :, col0:col0 + 128],
                                   ch="A" if slot == 0 else "B"))
            return toks[-1]

        wip_tok = {}
        wip_guard = {}
        wip_tok[0] = dma_wip(0, 0)

        attn_done_tok = 0
        sband_guard = {}
        pair_q_guard = {}
        c0_done = {}
        negc0 = {}
        f_ptr = 34

        for hp in range(8):
            slot = hp % 2
            base = PBUF * slot
            if hp + 1 < 8:
                wip_tok[hp + 1] = dma_wip(hp + 1, (hp + 1) % 2)
            ev_tok = None
            prev_use = pair_q_guard.get(('qk', slot), 0)
            for o in range(3):
                for n in range(4):
                    bi = 6 + (o * 4 + n) % 2
                    w0 = []
                    if o == 0 and n == 0:
                        w0 += ln_done + [wip_tok[hp]]
                        if prev_use:
                            w0.append(("pT", prev_use))
                    g = sband_guard.get(bi)
                    if g:
                        w0.append(g)
                    mm = None
                    for k in range(NC):
                        mm = em.op("T", lambda t, o=o, n=n, k=k, bi=bi, slot=slot:
                                   t.matmul(pkx(bi)[:, 0:512],
                                            Dd[:, WIPO + slot * 3072 + o * 1024 + k * 128:
                                               WIPO + slot * 3072 + o * 1024 + (k + 1) * 128],
                                            A[:, k * STRIDE + 1 + n * 512:k * STRIDE + 1 + (n + 1) * 512],
                                            start=(k == 0), stop=(k == NC - 1)),
                                   waits=(w0 if k == 0 else []), chain=False)
                    dst = Cc[:, base + o * 2048 + n * 512:base + o * 2048 + (n + 1) * 512]
                    if o == 2:
                        ev_tok = em.op("S", lambda s, dst=dst, bi=bi:
                                       s.copy(dst, pkx(bi)[:, 0:512]),
                                       waits=[("pT", mm)])
                    else:
                        ev_tok = em.op("S", lambda s, dst=dst, bi=bi:
                                       s.activation(dst, pkx(bi)[:, 0:512], AF.Identity,
                                                    scale=0.3535533905932738),
                                       waits=[("pT", mm)])
                    sband_guard[bi] = ("pS", ev_tok)
            qkv_done = ("pS", ev_tok)
            wip_guard[slot] = em.last["T"]

            fb = f_ptr
            b1 = em.op("V", lambda v, base=base, fb=fb:
                       v.bn_stats(f32s[:, fb:fb + 6], Cc[:, base + QV:base + QV + 512]),
                       waits=[qkv_done])
            b2 = em.op("V", lambda v, fb=fb:
                       v.bn_aggr(f32s[:, fb + 12:fb + 14], f32s[:, fb:fb + 6]))
            b3 = em.op("V", lambda v, base=base, fb=fb:
                       v.bn_stats(f32s[:, fb + 6:fb + 12], Cc[:, base + KV_:base + KV_ + 512]))
            b4 = em.op("V", lambda v, fb=fb:
                       v.bn_aggr(f32s[:, fb + 14:fb + 16], f32s[:, fb + 6:fb + 12]))
            b5 = em.op("V", lambda v, fb=fb:
                       v.scalar_tensor_tensor(f32s[:, fb + 16:fb + 17],
                                              f32s[:, fb + 12:fb + 13],
                                              f32s[:, fb + 12:fb + 13],
                                              f32s[:, fb + 13:fb + 14],
                                              ALU.mult, ALU.add))
            b6 = em.op("V", lambda v, fb=fb:
                       v.scalar_tensor_tensor(f32s[:, fb + 17:fb + 18],
                                              f32s[:, fb + 14:fb + 15],
                                              f32s[:, fb + 14:fb + 15],
                                              f32s[:, fb + 15:fb + 16],
                                              ALU.mult, ALU.add))
            c0g = sband_guard.get("c0")
            prev_e2 = None
            for h2 in range(2):
                head = 2 * hp + h2
                rows = slice(64 * h2, 64 * h2 + 64)
                half_ap = (cs["c_half64"][0:64, :] if h2 == 0
                           else cs["c_half64_hi"][64:128, :])
                av2g = sband_guard.get(("av", 0))
                mm1 = em.op("T", lambda t, fb=fb, rows=rows, half_ap=half_ap:
                            t.matmul(pk[2][0:1, 0:1], f32s[rows, fb + 16:fb + 17],
                                     half_ap, start=True, stop=False),
                            waits=[("pV", b6)] + ([c0g] if c0g else []) +
                                  ([av2g] if av2g else []) +
                                  ([("pV", prev_e2)] if prev_e2 else []), chain=False)
                mm2 = em.op("T", lambda t, fb=fb, rows=rows, half_ap=half_ap:
                            t.matmul(pk[2][0:1, 0:1], f32s[rows, fb + 17:fb + 18],
                                     half_ap, start=False, stop=True),
                            chain=False)
                e1 = em.op("S", lambda s, fb=fb, h2=h2:
                           s.activation(f32s[0:1, fb + 18 + h2:fb + 19 + h2],
                                        pk[2][0:1, 0:1], AF.Identity,
                                        bias=negmargin_c[0:1, :], scale=-1.0),
                           waits=[("pT", mm2)])
                mm3 = em.op("T", lambda t, fb=fb, h2=h2:
                            t.matmul(pk[2][0:128, 2 + h2:3 + h2], cs["c_ones_row"][:],
                                     f32s[0:1, fb + 18 + h2:fb + 19 + h2],
                                     start=True, stop=True),
                            waits=[("pS", e1)], chain=False)
                e2 = em.op("V", lambda v, fb=fb, h2=h2:
                           v.tensor_copy(f32s[:, fb + 20 + h2:fb + 21 + h2],
                                         pk[2][0:128, 2 + h2:3 + h2]),
                           waits=[("pT", mm3)])
                negc0[head] = (f32s[:, fb + 20 + h2:fb + 21 + h2], e2)
                prev_e2 = e2
                c0g = ("pV", e2)
            sband_guard["c0"] = c0g
            c0_last = ("pV", em.last["V"])
            f_ptr = 34 if f_ptr == 70 else 70

            vt_dst = VT + slot * 2080
            vtrT = 0
            for rr in range(4):
                bank = pk[4 + rr % 2]
                g = sband_guard.get(4 + rr % 2)
                tk = []
                for j in range(4):
                    c = rr * 4 + j
                    w = []
                    if j == 0:
                        w = [qkv_done] if rr == 0 else []
                        if rr == 0:
                            w.append(("pV", ones_tok))
                        if g:
                            w.append(g)
                    tk.append(em.op("T", lambda t, c=c, j=j, bank=bank, base=base:
                                    t.matmul(bank[:, 128 * j:128 * (j + 1)],
                                             Cc[:, base + VV + c * 128:base + VV + (c + 1) * 128],
                                             cs["c_ident"][:], start=True, stop=True),
                                    waits=w, chain=False))
                vtrT = tk[-1]
                evA = em.op("V", lambda v, rr=rr, bank=bank, vt_dst=vt_dst:
                            v.tensor_copy(
                                Dd[:, vt_dst + rr * 4 * 65:vt_dst + (rr * 4 + 4) * 65]
                                .rearrange("p (c f) -> p c f", f=65)[:, :, 0:64],
                                bank[:].rearrange("p (c f) -> p c f", f=128)[:, :, 0:64]),
                            waits=[("pT", tk[-1])])
                evB = em.op("V", lambda v, rr=rr, bank=bank, vt_dst=vt_dst:
                            v.tensor_copy(
                                Dd[:, vt_dst + 1040 + rr * 4 * 65:vt_dst + 1040 + (rr * 4 + 4) * 65]
                                .rearrange("p (c f) -> p c f", f=65)[:, :, 0:64],
                                bank[:].rearrange("p (c f) -> p c f", f=128)[:, :, 64:128]),
                            )
                sband_guard[4 + rr % 2] = ("pV", evB)
            vtok_done = ("pV", em.last["V"])

            PT4 = 10304
            sbmap = ((0, 6), (1, 7))
            for qc in range(4):
                nkc = 4 * (qc + 1)
                for h2 in range(2):
                    g = sband_guard.get(("av", h2))
                    w = [vtok_done]
                    if g:
                        w.append(g)
                    sband_guard[("avw", h2)] = w
                for kc in range(nkc + 1):
                    if kc < nkc:
                        qk_toks = {}
                        for h2 in range(2):
                            head = 2 * hp + h2
                            rows = slice(64 * h2, 64 * h2 + 64)
                            sbank = sbmap[h2][kc % 2]
                            gs = sband_guard.get(("s", sbank))
                            wqk = [qkv_done] if (qc == 0 and kc == 0) else []
                            if gs:
                                wqk.append(gs)
                            qk_toks[h2] = em.op("T", lambda t, kc=kc, qc=qc, sbank=sbank, base=base, rows=rows:
                                                t.matmul(pkx(sbank)[:, 0:512],
                                                         Cc[rows, base + KV_ + kc * 128:base + KV_ + (kc + 1) * 128],
                                                         Cc[rows, base + QV + qc * 512:base + QV + (qc + 1) * 512],
                                                         start=True, stop=True),
                                                waits=wqk, chain=False)
                        for h2 in range(2):
                            head = 2 * hp + h2
                            sbank = sbmap[h2][kc % 2]
                            pb = PT4 + (h2 * 2 + kc % 2) * 512
                            nc0_ap, nc0_tok = negc0[head]
                            we = [("pT", qk_toks[h2]), ("pV", nc0_tok)]
                            gp = sband_guard.get(("pt", h2, kc % 2))
                            if gp:
                                we.append(gp)
                            ex = em.op("S", lambda s, pb=pb, sbank=sbank, nc0_ap=nc0_ap:
                                       s.activation(Dd[:, pb:pb + 512], pkx(sbank)[:, 0:512],
                                                    AF.Exp, bias=nc0_ap, scale=1.0),
                                       waits=we)
                            sband_guard[("s", sbank)] = ("pS", ex)
                            p_ready = ("pS", ex)
                            m = kc - 4 * qc
                            if m >= 0:
                                mk = em.op("V", lambda v, pb=pb, m=m:
                                           v.tensor_mul(Dd[:, pb:pb + 512], Dd[:, pb:pb + 512],
                                                        cs["c_dband"][:, 384 - 128 * m:896 - 128 * m]),
                                           waits=[("pS", ex)])
                                p_ready = ("pV", mk)
                            sband_guard[("pr", h2, kc % 2)] = p_ready
                    if kc >= 1:
                        kcp = kc - 1
                        for h2 in range(2):
                            pb = PT4 + (h2 * 2 + kcp % 2) * 512
                            vt_base = vt_dst + 1040 * h2
                            av_mm = None
                            for sq in range(4):
                                w = []
                                if sq == 0:
                                    w = [sband_guard[("pr", h2, kcp % 2)]]
                                    if kcp == 0:
                                        w += sband_guard[("avw", h2)]
                                av_mm = em.op("T", lambda t, sq=sq, kcp=kcp, pb=pb, h2=h2, vt_base=vt_base, nkc=nkc:
                                              t.matmul(pk[2 + h2][:, 65 * sq:65 * sq + 65],
                                                       Dd[:, pb + 128 * sq:pb + 128 * (sq + 1)],
                                                       Dd[:, vt_base + 65 * kcp:vt_base + 65 * (kcp + 1)],
                                                       start=(kcp == 0 and sq == 0),
                                                       stop=(kcp == nkc - 1 and sq == 3)),
                                              waits=w, chain=False)
                            sband_guard[("pt", h2, kcp % 2)] = ("pT", av_mm)
                            sband_guard[("avl", h2)] = ("pT", av_mm)
                for h2 in range(2):
                    head = 2 * hp + h2
                    rc = em.op("V", lambda v, h2=h2: v.reciprocal(
                        f32s[:, 24 + 4 * h2:28 + 4 * h2],
                        pk[2 + h2][:, 0:260].rearrange("p (s f) -> p s f", f=65)[:, :, 64:65]),
                        waits=[sband_guard[("avl", h2)]])
                    last_ep = rc
                    for sq in range(4):
                        i_tok = qc * 4 + sq
                        last_ep = em.op("V", lambda v, sq=sq, h2=h2, i_tok=i_tok, head=head:
                                        v.tensor_scalar(
                                            Bb[:, i_tok * 1024 + 64 * head:i_tok * 1024 + 64 * head + 64],
                                            pk[2 + h2][:, 65 * sq:65 * sq + 64],
                                            f32s[:, 24 + 4 * h2 + sq:25 + 4 * h2 + sq], None, ALU.mult))
                    sband_guard[("av", h2)] = ("pV", last_ep)
            pair_q_guard[("qk", slot)] = em.last["T"]
        attn_done = [("pV", em.last["V"]), ("pT", em.last["T"]), ("pS", em.last["S"])]

        lt2 = None
        ltT2 = 0
        for i in range(NT):
            lt2, ltT2 = emit_transpose_1024(
                lambda j, i=i: Bb[:, i * 1024 + 128 * j:i * 1024 + 128 * (j + 1)],
                lambda j, i=i: A[:, j * STRIDE + 1 + i * 128:j * STRIDE + 1 + (i + 1) * 128],
                "ptr", wait_src=(attn_done if i == 0 else []))
        atr_done = [("pV", lt2), ("pT", ltT2)]

        em.waits("G", [("pT", em.last["T"]), ("pS", em.last["S"]), ("pV", em.last["V"])])
        wo_tok = None
        for k in range(NC):
            dst = Dd[:, k * 1024:(k + 1) * 1024]
            wo_tok = em.dma("G", dst, D["woaT"][128 * k:128 * (k + 1), :], ch="W")
        wt = None
        for k in range(NC):
            wt = em.dma("G", Dd[:, 8192 + k * 1024:8192 + (k + 1) * 1024],
                        D["wkT"][128 * k:128 * (k + 1), :], ch="W")
        wpf["wk"] = wt
        for k in range(NC):
            wt = em.dma("G", Cc[:, 8192 + k * 1024:8192 + (k + 1) * 1024],
                        D["wvT"][128 * k:128 * (k + 1), :], ch="W")
        wpf["wv"] = wt
        o_guard = {}
        for ii in range(NT):
            for fc in range(2):
                bi = (ii * 2 + fc) % 2
                w0 = []
                if ii == 0 and fc == 0:
                    w0 += atr_done + [wo_tok]
                g = o_guard.get(bi)
                if g:
                    w0.append(("pV", g))
                mm = None
                for k in range(NC):
                    mm = em.op("T", lambda t, k=k, ii=ii, fc=fc, bi=bi:
                               t.matmul(pk[bi][:, 0:512],
                                        A[:, k * STRIDE + 1 + ii * 128:k * STRIDE + 1 + (ii + 1) * 128],
                                        Dd[:, k * 1024 + fc * 512:k * 1024 + (fc + 1) * 512],
                                        start=(k == 0), stop=(k == NC - 1)),
                               waits=(w0 if k == 0 else []), chain=False)
                o_guard[bi] = em.op("V", lambda v, ii=ii, fc=fc, bi=bi:
                                    v.tensor_add(x_sb[:, ii * 1024 + fc * 512:ii * 1024 + (fc + 1) * 512],
                                                 x_sb[:, ii * 1024 + fc * 512:ii * 1024 + (fc + 1) * 512],
                                                 pk[bi][:, 0:512]),
                                    waits=[("pT", mm)])
        em.waits("G", [("pT", em.last["T"])])
        for k in range(NC):
            wt = em.dma("G", Dd[:, k * 1024:(k + 1) * 1024],
                        D["wrT"][128 * k:128 * (k + 1), :], ch="W")
        wpf["wr"] = wt
        em.barrier(exclude=("dGW",))

    def emit_ffn():
        w_r = lambda k, f0: Dd[:, k * 1024 + f0:k * 1024 + f0 + 512]
        w_k = lambda k, fb: Dd[:, 8192 + k * 1024 + fb * 128:8192 + k * 1024 + (fb + 1) * 128]
        w_v = lambda k, f0: Cc[:, 8192 + k * 1024 + f0:8192 + k * 1024 + f0 + 512]
        if "wr" in wpf:
            wr_tok = wpf.pop("wr")
            wk_tok = wpf.pop("wk")
            wv_tok = wpf.pop("wv")
        else:
            wt = None
            for k in range(NC):
                wt = em.dma("G", Dd[:, k * 1024:(k + 1) * 1024],
                            D["wrT"][128 * k:128 * (k + 1), :], ch="W")
            wr_tok = wt
            for k in range(NC):
                wt = em.dma("G", Dd[:, 8192 + k * 1024:8192 + (k + 1) * 1024],
                            D["wkT"][128 * k:128 * (k + 1), :], ch="W")
            wk_tok = wt
            for k in range(NC):
                wt = em.dma("G", Cc[:, 8192 + k * 1024:8192 + (k + 1) * 1024],
                            D["wvT"][128 * k:128 * (k + 1), :], ch="W")
            wv_tok = wt
        w_ready = [wr_tok]
        if ffn_stage <= 0.2:
            em.barrier()
            return

        for k in range(NC):
            em.op("V", lambda v, k=k: v.memset(A[:, k * STRIDE:k * STRIDE + 1], 0.0))
        if ffn_stage <= 0.4:
            em.barrier()
            return
        lt, last_tr_T = emit_ln_pass_batched(
            list(range(NT)),
            lambda ii, j: A[:, j * STRIDE + 1 + ii * 128:j * STRIDE + 1 + (ii + 1) * 128],
            0)
        u_done = [("pV", lt), ("pT", last_tr_T)]
        if ffn_stage <= 1:
            em.barrier()
            return

        r_banks = (0, 1)
        ka_banks = (2, 3)
        o_banks = (4, 5)
        r_ev = {}
        ka_ev = {}
        o_ev = {}
        for n in range(4):
            for k in range(NC):
                u_ap = A[:, k * STRIDE + 1 + n * 512:k * STRIDE + 1 + n * 512 + 512]
                up_ap = A[:, k * STRIDE + 0 + n * 512:k * STRIDE + 0 + n * 512 + 512]
                dl = Cc[:, k * 512:(k + 1) * 512]
                xk_ap = Cc[:, 4096 + k * 512:4608 + k * 512]
                em.op("V", lambda v, u_ap=u_ap, up_ap=up_ap, dl=dl: v.tensor_sub(dl, u_ap, up_ap),
                      waits=(u_done if (n == 0 and k == 0) else []))
                em.op("V", lambda v, dl=dl, up_ap=up_ap, xk_ap=xk_ap, k=k:
                      v.scalar_tensor_tensor(xk_ap, dl, pcols["muk_col"][:, k:k + 1],
                                             up_ap, ALU.mult, ALU.add))
                em.op("V", lambda v, dl=dl, up_ap=up_ap, k=k:
                      v.scalar_tensor_tensor(dl, dl, pcols["mur_col"][:, k:k + 1],
                                             up_ap, ALU.mult, ALU.add))
            mixed = [("pV", em.last["V"])]
            if ffn_stage <= 2:
                continue

            for ii in range(4):
                for fc in range(2):
                    bi = r_banks[(ii * 2 + fc) % 2]
                    w0 = list(mixed)
                    if n == 0 and ii == 0 and fc == 0:
                        w0 += w_ready
                    if bi in r_ev:
                        w0.append(("pS", r_ev[bi]))
                    mm = None
                    for k in range(NC):
                        mm = em.op("T", lambda t, k=k, ii=ii, fc=fc, bi=bi:
                                   t.matmul(pk[bi][:, 0:512],
                                            Cc[:, k * 512 + ii * 128:k * 512 + (ii + 1) * 128],
                                            w_r(k, fc * 512),
                                            start=(k == 0), stop=(k == NC - 1)),
                                   waits=(w0 if k == 0 else []), chain=False)
                    r_ev[bi] = em.op("S", lambda s, bi=bi, ii=ii, fc=fc:
                                     s.activation(Bb[:, ii * 1024 + fc * 512:ii * 1024 + fc * 512 + 512],
                                                  pk[bi][:, 0:512], AF.Sigmoid),
                                     waits=[("pT", mm)])
            r_done = ("pS", em.last["S"])
            if ffn_stage <= 3:
                continue

            for fb in range(NC):
                bi = ka_banks[fb % 2]
                w0 = list(mixed)
                if n == 0 and fb == 0:
                    w0.append(wk_tok)
                if bi in ka_ev:
                    w0.append(("pS", ka_ev[bi]))
                mm = None
                for k in range(NC):
                    mm = em.op("T", lambda t, k=k, fb=fb, bi=bi:
                               t.matmul(pk[bi][:, 0:512], w_k(k, fb),
                                        Cc[:, 4096 + k * 512:4608 + k * 512],
                                        start=(k == 0), stop=(k == NC - 1)),
                               waits=(w0 if k == 0 else []), chain=False)
                ka_ev[bi] = em.op("S", lambda s, bi=bi, fb=fb:
                                  s.activation(Bb[:, 4096 + fb * 512:4608 + fb * 512],
                                               pk[bi][:, 0:512], AF.Relu),
                                  waits=[("pT", mm)])
                em.op("V", lambda v, fb=fb:
                      v.tensor_mul(Bb[:, 4096 + fb * 512:4608 + fb * 512],
                                   Bb[:, 4096 + fb * 512:4608 + fb * 512],
                                   Bb[:, 4096 + fb * 512:4608 + fb * 512]),
                      waits=[("pS", ka_ev[bi])])
            ka_done = ("pV", em.last["V"])
            if ffn_stage <= 4:
                continue

            for ii in range(4):
                i = 4 * n + ii
                for dc in range(2):
                    bi = o_banks[(ii * 2 + dc) % 2]
                    w0 = [ka_done]
                    if n == 0 and ii == 0 and dc == 0:
                        w0.append(wv_tok)
                    if bi in o_ev:
                        w0.append(("pV", o_ev[bi]))
                    mm = None
                    for k in range(NC):
                        mm = em.op("T", lambda t, k=k, ii=ii, dc=dc, bi=bi:
                                   t.matmul(pk[bi][:, 0:512],
                                            Bb[:, 4096 + k * 512 + ii * 128:4096 + k * 512 + (ii + 1) * 128],
                                            w_v(k, dc * 512),
                                            start=(k == 0), stop=(k == NC - 1)),
                                   waits=(w0 if k == 0 else []), chain=False)
                    t1 = em.op("V", lambda v, bi=bi, ii=ii, dc=dc:
                               v.tensor_mul(f32s[:, 288:800],
                                            pk[bi][:, 0:512],
                                            Bb[:, ii * 1024 + dc * 512:ii * 1024 + dc * 512 + 512]),
                               waits=[("pT", mm), r_done])
                    o_ev[bi] = t1
                    em.op("V", lambda v, i=i, dc=dc:
                          v.tensor_add(x_sb[:, i * 1024 + dc * 512:i * 1024 + dc * 512 + 512],
                                       x_sb[:, i * 1024 + dc * 512:i * 1024 + dc * 512 + 512],
                                       f32s[:, 288:800]))
        em.barrier()


    def emit_gb_rebuild(g_name, b_name, stage_off, gate):
        em.waits("G", gate)
        em.dma("G", Dd[0:1, stage_off:stage_off + 1024], D[g_name], ch="W")
        d2 = em.dma("G", Dd[0:1, stage_off + 1024:stage_off + 2048], D[b_name], ch="W")
        tok = None
        prev = {}
        for i, dst in enumerate([g_bc, b_bc]):
            for h2 in range(2):
                bi = (i * 2 + h2) % 2
                w = [d2] + list(gate)
                if bi in prev:
                    w.append(("pV", prev[bi]))
                mm = em.op("T", lambda t, i=i, h2=h2, bi=bi, stage_off=stage_off: t.matmul(
                    pk[bi][:, 0:512], cs["c_ones_row_bf"][:],
                    Dd[0:1, stage_off + i * 1024 + h2 * 512:stage_off + i * 1024 + h2 * 512 + 512],
                    start=True, stop=True), waits=w, chain=False)
                tok = em.op("V", lambda v, dst=dst, h2=h2, bi=bi:
                            v.tensor_copy(dst[:, h2 * 512:h2 * 512 + 512], pk[bi][:, 0:512]),
                            waits=[("pT", mm)])
                prev[bi] = tok
        return ("pV", tok)

    def emit_attn_b():
        STM, STB, QTOK, KTOK, VS = 0, 2048, 4096, 5120, 6144
        KVEC, QVEC = 7168, 7200
        SICOL, SOCOL, ECOL, CKCQ, CS2 = 7232, 7248, 7264, 7280, 7312
        KST3, QST3, SROW = 7344, 7392, 7440
        AKV = 1026
        UCM, ATK = 8192, 8192
        pPACD, pSCAN, pOUT, pKV, pSM = pk[0], pk[1], pk[2], pk[3], pk6
        pOUT2 = pk7
        FB, FC = 400, 448
        EV1, EV2 = 608, 768

        def ecol(c0):
            return f32s[:, EV1:EV1 + 160].rearrange("p (h c) -> p h c", c=10)[:, :, c0]
        ec2 = f32s[:, EV2:EV2 + 32].rearrange("p (h c) -> p h c", c=2)

        guards = {}

        em.op("V", lambda v: v.memset(A[:, AKV:AKV + 1024], 0.0))
        for base in (KST3, QST3):
            em.op("V", lambda v, base=base: v.memset(Dd[:, base:base + 48], 0.0))
            em.op("V", lambda v, base=base: v.memset(
                Dd[0:64, base:base + 48].rearrange("p (h c) -> p h c", c=6)[:, :, 1], 1.0))
            em.op("V", lambda v, base=base: v.memset(
                Dd[64:128, base:base + 48].rearrange("p (h c) -> p h c", c=6)[:, :, 4], 1.0))
        em.op("V", lambda v: v.memset(Dd[0:1, SROW:SROW + 80], 0.0))
        em.op("V", lambda v: v.memset(Dd[:, KVEC:KVEC + 32], 1.0))
        em.op("V", lambda v: v.memset(Dd[:, QVEC:QVEC + 32], 1.0))
        em.op("V", lambda v: v.memset(
            Dd[:, KVEC:KVEC + 32].rearrange("p (h c) -> p h c", c=2)[:, :, 1], 0.0))
        em.op("V", lambda v: v.memset(
            Dd[:, QVEC:QVEC + 32].rearrange("p (h c) -> p h c", c=2)[:, :, 1], 0.0))
        st_init = em.op("V", lambda v: v.memset(Dd[:, VS:VS + 512], 0.0))
        init_done = [("pV", st_init)]

        def qslc(h, lt):
            p, h2 = h // 2, h % 2
            return Bb[64 * h2:64 * h2 + 64, p * 1024 + lt * 128:p * 1024 + (lt + 1) * 128]

        def kslc(h, lt):
            p, h2 = h // 2, h % 2
            return Bb[64 * h2:64 * h2 + 64, 8192 + p * 1024 + lt * 128:8192 + p * 1024 + (lt + 1) * 128]

        def stm(h):
            return Dd[:, STM + h * 128:STM + (h + 1) * 128]

        def stb(h):
            return Dd[:, STB + h * 128:STB + (h + 1) * 128]


        def emit_ln_half(half):
            lt_v, lt_T = emit_ln_pass_batched(
                list(range(8 * half, 8 * half + 8)),
                lambda ii, j: Cc[:, UCM + j * 1024 + (ii - 8 * half) * 128:
                                 UCM + j * 1024 + (ii - 8 * half + 1) * 128],
                0, gdict=guards)
            return [("pV", lt_v), ("pT", lt_T)]


        def proj_qk(which, dst_base, bias_col, u_done_):
            wname = "wq" if which == "q" else "wk"
            wcs = D[wname].rearrange("(k p) f -> p k f", p=128)
            em.waits("G", [w for w in u_done_ if w[0] in ("pT", "pV", "pS")])
            evs = None
            for fb in range(NC):
                slot = fb % 2
                g = guards.get(("pw", slot))
                if g:
                    em.waits("G", [g])
                dt_ = em.dma("G", Dd[:, STM + slot * 1024:STM + (slot + 1) * 1024]
                             .rearrange("p (k f) -> p k f", f=128),
                             wcs[:, :, 128 * fb:128 * (fb + 1)],
                             ch=("A" if slot == 0 else "B"))
                last_mm = None
                for tc in range(2):
                    bi = 4 + (fb * 2 + tc) % 2
                    gb = guards.get(bi)
                    w0 = [dt_] + list(u_done_)
                    if gb:
                        w0.append(gb)
                    mm = None
                    for k in range(NC):
                        mm = em.op("T", lambda t, k=k, tc=tc, bi=bi, slot=slot:
                                   t.matmul(pk[bi][:, 0:512],
                                            Dd[:, STM + slot * 1024 + k * 128:STM + slot * 1024 + (k + 1) * 128],
                                            Cc[:, UCM + k * 1024 + tc * 512:UCM + k * 1024 + (tc + 1) * 512],
                                            start=(k == 0), stop=(k == NC - 1)),
                                   waits=(w0 if k == 0 else []), chain=False)
                    evs = em.op("S", lambda s, fb=fb, tc=tc, bi=bi, dst_base=dst_base, bias_col=bias_col:
                                s.activation(Bb[:, dst_base + fb * 1024 + tc * 512:dst_base + fb * 1024 + (tc + 1) * 512],
                                             pk[bi][:, 0:512], AF.Sigmoid,
                                             bias=bias_col[:, fb:fb + 1], scale=1.0),
                                waits=[("pT", mm)])
                    guards[bi] = ("pS", evs)
                    last_mm = mm
                guards[("pw", slot)] = ("pT", last_mm)
            return ("pS", evs)

        def proj_v(u_done_):
            em.waits("G", [w for w in u_done_ if w[0] in ("pT", "pV", "pS")])
            for sl in (0, 1):
                g = guards.get(("pw", sl))
                if g:
                    em.waits("G", [g])
            bv_d = em.dma("G", Dd[0:1, ATK:ATK + 1024], D["bv_row"], ch="W")
            evs = None
            for dc in range(2):
                g = guards.get(("pv", dc))
                if g:
                    em.waits("G", [g])
                vst = 0 if dc == 0 else ATK + 1024
                dt_ = em.dma("G", Dd[:, vst:vst + 4096]
                             .rearrange("p (k f) -> p k f", f=512),
                             D["wv"].rearrange("(k p) f -> p k f", p=128)[:, :, dc * 512:(dc + 1) * 512],
                             ch=("A" if dc == 0 else "B"))
                last_mm = None
                for tt in range(8):
                    bi = 4 + (dc * 8 + tt) % 2
                    gb = guards.get(bi)
                    w0 = [dt_, bv_d] + list(u_done_)
                    if gb:
                        w0.append(gb)
                    mm = None
                    for k in range(NC):
                        mm = em.op("T", lambda t, k=k, tt=tt, dc=dc, bi=bi,
                                   vst=(0 if dc == 0 else ATK + 1024):
                                   t.matmul(pk[bi][:, 0:512],
                                            Cc[:, UCM + k * 1024 + tt * 128:UCM + k * 1024 + (tt + 1) * 128],
                                            Dd[:, vst + k * 512:vst + (k + 1) * 512],
                                            start=(k == 0), stop=False),
                                   waits=(w0 if k == 0 else []), chain=False)
                    mm = em.op("T", lambda t, dc=dc, bi=bi:
                               t.matmul(pk[bi][:, 0:512], cs["c_ones_row_bf"][:],
                                        Dd[0:1, ATK + dc * 512:ATK + (dc + 1) * 512],
                                        start=False, stop=True), chain=False)
                    evs = em.op("S", lambda s, tt=tt, dc=dc, bi=bi:
                                s.copy(Cc[:, tt * 1024 + dc * 512:tt * 1024 + (dc + 1) * 512],
                                       pk[bi][:, 0:512]),
                                waits=[("pT", mm)])
                    guards[bi] = ("pS", evs)
                    last_mm = mm
                guards[("pv", dc)] = ("pT", last_mm)
            return ("pS", evs)


        def scan_chunk(gc, qkv_done):
            lt = gc % 8
            lp_col = cs["c_lpos"][:, gc:gc + 1]
            rlp_col = cs["c_rlpos"][:, gc:gc + 1]
            if gc == 0:
                ow = [w for w in [guards.get("SMrd"), guards.get("KVrd")] if w]
                em.op("T", lambda t: t.matmul(pKV[:, 0:512], cs["c_ident"][:],
                                              Dd[:, VS:VS + 512], start=True,
                                              stop=False, skip_group_check=True),
                      waits=ow + init_done + list(qkv_done), chain=False)
                op2 = em.op("T", lambda t: t.matmul(pSM[:, 0:512], cs["c_ident"][:],
                                                    Dd[:, VS:VS + 512], start=True,
                                                    stop=False, skip_group_check=True),
                            chain=False)
                guards["VSb"] = ("pT", op2)

            def pairt(h, off=0):
                p = h // 2
                return Bb[:, off + p * 1024 + lt * 128:off + p * 1024 + (lt + 1) * 128]

            tr_ev = None
            for grp in range(4):
                bank = pk[4 + grp % 2]
                g = guards.get(4 + grp % 2)
                tk = None
                for j in range(4):
                    idx = grp * 4 + j
                    p = idx % 8
                    off = 0 if idx < 8 else 8192
                    src = Bb[:, off + p * 1024 + lt * 128:off + p * 1024 + (lt + 1) * 128]
                    w = []
                    if j == 0:
                        w = list(qkv_done) + init_done if grp == 0 else []
                        if g:
                            w.append(g)
                    tk = em.op("T", lambda t, src=src, j=j, bank=bank:
                               t.matmul(bank[:, 128 * j:128 * (j + 1)], src,
                                        cs["c_ident"][:], start=True, stop=True),
                               waits=w, chain=False)
                tr_ev = em.op("V", lambda v, grp=grp, bank=bank:
                              v.tensor_copy(Dd[:, QTOK + grp * 512:QTOK + (grp + 1) * 512],
                                            bank[:, 0:512]),
                              waits=[("pT", tk)])
                guards[4 + grp % 2] = ("pV", tr_ev)
            tr_done = ("pV", tr_ev)
            if DBG_SCAN == "tr":
                return

            stm_ev = None
            for grp in range(8):
                isST = grp < 4
                par = grp % 2
                half4 = (grp // 2) % 2
                bank = pk[4 + par]
                g = guards.get(4 + par)
                heads = [par + 2 * (4 * half4 + j) for j in range(4)]
                tk = None
                for j, h in enumerate(heads):
                    r0 = 64 * par
                    lh = (kslc(h, lt) if isST else qslc(h, lt))
                    rh = (qslc(h, lt) if isST else kslc(h, lt))
                    w = []
                    if j == 0:
                        w = [tr_done] if grp == 0 else []
                        if g:
                            w.append(g)
                    tk = em.op("T", lambda t, lh=lh, rh=rh, j=j, bank=bank:
                               t.matmul(bank[:, 128 * j:128 * (j + 1)], lh, rh,
                                        start=True, stop=True),
                               waits=w, chain=False)
                dst0 = STM if isST else STB
                h0 = par + 8 * half4
                stm_ev = em.op("V", lambda v, h0=h0, bank=bank, dst0=dst0:
                               v.tensor_mul(
                                   Dd[:, dst0:dst0 + 2048]
                                   .rearrange("p (h c) -> p h c", c=128)[:, h0:min(h0 + 8, 16):2, :],
                                   bank[:, 0:512].rearrange("p (j c) -> p j c", c=128),
                                   cs["c_triu4"][:].rearrange("p (j c) -> p j c", c=128)),
                               waits=[("pT", tk)])
                guards[4 + par] = ("pV", stm_ev)
            stm_done = ("pV", stm_ev)
            if DBG_SCAN == "st":
                return

            gA = guards.get("PACD")
            gsr = guards.get("SMrd")
            mmA = None
            for h in range(16):
                w = [stm_done]
                if h == 0:
                    if gA:
                        w.append(gA)
                    if gsr:
                        w.append(gsr)
                mmA = em.op("T", lambda t, h=h, lt=lt:
                            t.matmul(pPACD[:, 10 * h + 4:10 * h + 7], pairt(h),
                                     Dd[:, KST3 + 3 * h:KST3 + 3 * h + 3],
                                     start=(h == 0), stop=False, skip_group_check=True),
                            waits=w, chain=False)
                mmA = em.op("T", lambda t, h=h, lt=lt:
                            t.matmul(pPACD[:, 10 * h + 7:10 * h + 10], pairt(h, 8192),
                                     Dd[:, QST3 + 3 * h:QST3 + 3 * h + 3],
                                     start=False, stop=False, skip_group_check=True),
                            chain=False)
                mmA = em.op("T", lambda t, h=h:
                            t.matmul(pPACD[:, 10 * h:10 * h + 1], stm(h), cs["c_ones_col"][:],
                                     start=False, stop=False, skip_group_check=True),
                            chain=False)
                mmA = em.op("T", lambda t, h=h:
                            t.matmul(pPACD[:, 10 * h + 1:10 * h + 2], stb(h), cs["c_ones_col"][:],
                                     start=False, stop=False, skip_group_check=True),
                            chain=False)
            phA = ("pT", mmA)
            if DBG_SCAN == "phA":
                return

            def pcol(c0):
                return pPACD[:, 0:160].rearrange("p (h c) -> p h c", c=10)[:, :, c0]

            cpk = em.op("V", lambda v: v.tensor_copy(Dd[:, CKCQ:CKCQ + 16], pcol(8)),
                        waits=[phA])
            cpq = em.op("V", lambda v: v.tensor_copy(Dd[:, CKCQ + 16:CKCQ + 32], pcol(5)))
            gS = guards.get("SCAN")
            w = [("pV", cpq)] + ([gS] if gS else [])
            em.op("T", lambda t: t.matmul(pSCAN[:, 0:16], cs["c_triu"][:],
                                          Dd[:, CKCQ:CKCQ + 16], start=True, stop=False),
                  waits=w, chain=False)
            em.op("T", lambda t: t.matmul(pSCAN[:, 0:16], cs["c_ones_row_bf"][:],
                                          Dd[0:1, SROW:SROW + 16], start=False, stop=False),
                  chain=False)
            em.op("T", lambda t: t.matmul(pSCAN[:, 16:32], cs["c_triu"][:],
                                          Dd[:, CKCQ + 16:CKCQ + 32], start=False, stop=False),
                  chain=False)
            m4 = em.op("T", lambda t: t.matmul(pSCAN[:, 16:32], cs["c_ones_row_bf"][:],
                                               Dd[0:1, SROW + 16:SROW + 32],
                                               start=False, stop=True), chain=False)
            scA = ("pT", m4)
            gsm = guards.get("SMmm")
            w = [("pV", cpq)] + ([gsm] if gsm else []) + ([gsr] if gsr else [])
            em.op("T", lambda t: t.matmul(pSM[0:1, 100:116], cs["c_ones_col"][:],
                                          Dd[:, CKCQ:CKCQ + 16],
                                          start=False, stop=False,
                                          skip_group_check=True),
                  waits=w, chain=False)
            em.op("T", lambda t: t.matmul(pSM[0:1, 116:132], cs["c_ones_col"][:],
                                          Dd[:, CKCQ + 16:CKCQ + 32],
                                          start=False, stop=False,
                                          skip_group_check=True), chain=False)

            ev1 = em.op("V", lambda v: v.tensor_copy(f32s[:, EV1:EV1 + 160],
                                                     pPACD[:, 0:160]),
                        waits=[phA])
            em.op("V", lambda v: v.tensor_add(
                f32s[:, FB + 32:FB + 48], ecol(0), ecol(4)))
            em.op("V", lambda v: v.tensor_add(
                f32s[:, FB + 48:FB + 64], ecol(1), ecol(7)))
            em.op("V", lambda v: v.tensor_add(
                f32s[:, FB + 64:FB + 80], ecol(5), pSCAN[:, 0:16]), waits=[scA])
            a2 = em.op("V", lambda v: v.tensor_add(
                f32s[:, FB + 80:FB + 96], ecol(8), pSCAN[:, 16:32]))
            guards["SCAN"] = ("pV", a2)
            em.op("V", lambda v: v.scalar_tensor_tensor(
                f32s[:, FB:FB + 32], f32s[:, FB + 64:FB + 96], EPS_B,
                f32s[:, FB + 32:FB + 64], ALU.mult, ALU.add))
            em.op("V", lambda v, rlp_col=rlp_col: v.tensor_scalar(
                f32s[:, FB:FB + 32], f32s[:, FB:FB + 32], 64.0 * EPS_B * EPS_B,
                rlp_col, ALU.add, ALU.mult))
            em.op("V", lambda v: v.reciprocal(f32s[:, FB:FB + 32], f32s[:, FB:FB + 32]))
            em.op("V", lambda v: v.tensor_copy(
                Dd[:, QVEC:QVEC + 32].rearrange("p (h c) -> p h c", c=2)[:, :, 1],
                f32s[:, FB:FB + 16]))
            c4 = em.op("V", lambda v: v.tensor_copy(
                Dd[:, KVEC:KVEC + 32].rearrange("p (h c) -> p h c", c=2)[:, :, 1],
                f32s[:, FB + 16:FB + 32]))
            sis_done = ("pV", c4)
            if DBG_SCAN == "phB":
                return

            mmC = None
            for h in range(16):
                w = [sis_done] if h == 0 else []
                mmC = em.op("T", lambda t, h=h:
                            t.matmul(pPACD[:, 10 * h + 2:10 * h + 3], stm(h),
                                     Dd[:, KVEC + 2 * h + 1:KVEC + 2 * h + 2],
                                     start=False, stop=False, skip_group_check=True),
                            waits=w, chain=False)
                mmC = em.op("T", lambda t, h=h:
                            t.matmul(pPACD[:, 10 * h + 3:10 * h + 4], stb(h),
                                     Dd[:, QVEC + 2 * h + 1:QVEC + 2 * h + 2],
                                     start=False, stop=False, skip_group_check=True),
                            chain=False)
            phC = ("pT", mmC)
            if DBG_SCAN == "phC":
                return

            em.op("V", lambda v: v.tensor_mul(
                Dd[:, CS2:CS2 + 16], Dd[:, CKCQ:CKCQ + 16],
                Dd[:, KVEC:KVEC + 32].rearrange("p (h c) -> p h c", c=2)[:, :, 1]))
            x2 = em.op("V", lambda v: v.tensor_mul(
                Dd[:, CS2 + 16:CS2 + 32], Dd[:, CKCQ + 16:CKCQ + 32],
                Dd[:, QVEC:QVEC + 32].rearrange("p (h c) -> p h c", c=2)[:, :, 1]))
            gS = guards.get("SCAN")
            w = [("pV", x2)] + ([gS] if gS else [])
            em.op("T", lambda t: t.matmul(pSCAN[:, 32:48], cs["c_triu"][:],
                                          Dd[:, CS2:CS2 + 16], start=True, stop=False),
                  waits=w, chain=False)
            em.op("T", lambda t: t.matmul(pSCAN[:, 32:48], cs["c_ones_row_bf"][:],
                                          Dd[0:1, SROW + 32:SROW + 48],
                                          start=False, stop=False), chain=False)
            em.op("T", lambda t: t.matmul(pSCAN[:, 48:64], cs["c_triu"][:],
                                          Dd[:, CS2 + 16:CS2 + 32], start=False, stop=False),
                  chain=False)
            n4 = em.op("T", lambda t: t.matmul(pSCAN[:, 48:64], cs["c_ones_row_bf"][:],
                                               Dd[0:1, SROW + 48:SROW + 64],
                                               start=False, stop=True), chain=False)
            scC = ("pT", n4)
            em.op("T", lambda t: t.matmul(pSM[0:1, 132:148], cs["c_ones_col"][:],
                                          Dd[:, CS2:CS2 + 16],
                                          start=False, stop=False, skip_group_check=True),
                  waits=[("pV", x2)], chain=False)
            em.op("T", lambda t: t.matmul(pSM[0:1, 148:164], cs["c_ones_col"][:],
                                          Dd[:, CS2 + 16:CS2 + 32],
                                          start=False, stop=False, skip_group_check=True),
                  chain=False)

            ev2 = em.op("V", lambda v: v.tensor_copy(
                f32s[:, EV2:EV2 + 32].rearrange("p (h c) -> p h c", c=2),
                pPACD[:, 0:160].rearrange("p (h c) -> p h c", c=10)[:, :, 2:4]),
                waits=[phC])
            guards["PACD"] = ("pV", ev2)
            b0 = em.op("V", lambda v: v.tensor_add(
                f32s[:, FC:FC + 16], ec2[:, :, 0], ecol(6)))
            b0b = em.op("V", lambda v: v.tensor_add(
                f32s[:, FC + 16:FC + 32], ec2[:, :, 1], ecol(9)))
            em.op("V", lambda v: v.tensor_add(
                f32s[:, FC + 96:FC + 112], ecol(5), pSCAN[:, 32:48]), waits=[scC])
            b2 = em.op("V", lambda v: v.tensor_add(
                f32s[:, FC + 112:FC + 128], ecol(8), pSCAN[:, 48:64]))
            guards["SCAN"] = ("pV", b2)
            em.op("V", lambda v: v.scalar_tensor_tensor(
                f32s[:, FC:FC + 32], f32s[:, FC + 96:FC + 128], EPS_B,
                f32s[:, FC:FC + 32], ALU.mult, ALU.add))
            em.op("V", lambda v, rlp_col=rlp_col: v.tensor_scalar(
                f32s[:, FC:FC + 32], f32s[:, FC:FC + 32], 64.0 * EPS_B * EPS_B,
                rlp_col, ALU.add, ALU.mult))
            b4 = em.op("V", lambda v: v.tensor_scalar(
                f32s[:, FC + 16:FC + 32], f32s[:, FC + 16:FC + 32], 1.0, -1.0,
                ALU.min, ALU.max))
            s1 = em.op("S", lambda s: s.activation(f32s[:, FC + 32:FC + 48],
                                                   f32s[:, FC:FC + 16], AF.Exp,
                                                   scale=-1.0), waits=[("pV", b4)])
            s2 = em.op("S", lambda s: s.activation(f32s[:, FC + 48:FC + 64],
                                                   f32s[:, FC + 16:FC + 32], AF.Exp))
            em.op("V", lambda v: v.tensor_scalar(
                f32s[:, FC + 32:FC + 48], f32s[:, FC + 32:FC + 48], 1.0, None, ALU.add),
                waits=[("pS", s1)])
            em.op("V", lambda v: v.reciprocal(f32s[:, FC + 32:FC + 48],
                                              f32s[:, FC + 32:FC + 48]))
            c5 = em.op("V", lambda v: v.tensor_copy(Dd[:, ECOL:ECOL + 16],
                                                    f32s[:, FC + 48:FC + 64]),
                       waits=[("pS", s2)])
            gS = guards.get("SCAN")
            w = [("pV", c5)] + ([gS] if gS else [])
            em.op("T", lambda t: t.matmul(pSCAN[:, 64:80], cs["c_triu"][:],
                                          Dd[:, ECOL:ECOL + 16], start=True, stop=False),
                  waits=w, chain=False)
            e2 = em.op("T", lambda t: t.matmul(pSCAN[:, 64:80], cs["c_ones_row_bf"][:],
                                               Dd[0:1, SROW + 64:SROW + 80],
                                               start=False, stop=True), chain=False)
            em.op("T", lambda t: t.matmul(pSM[0:1, 164:180], cs["c_ones_col"][:],
                                          Dd[:, ECOL:ECOL + 16],
                                          start=False, stop=False, skip_group_check=True),
                  waits=[("pV", c5)], chain=False)
            d1 = em.op("V", lambda v: v.reciprocal(f32s[:, FC + 64:FC + 80],
                                                   pSCAN[:, 64:80]),
                       waits=[("pT", e2)])
            guards["SCAN"] = ("pV", d1)
            d3 = em.op("V", lambda v, lp_col=lp_col: v.scalar_tensor_tensor(
                f32s[:, FC + 64:FC + 80], f32s[:, FC + 48:FC + 64], lp_col,
                f32s[:, FC + 64:FC + 80], ALU.mult, ALU.mult))
            d5 = em.op("V", lambda v, rlp_col=rlp_col: v.scalar_tensor_tensor(
                f32s[:, FC + 80:FC + 96], f32s[:, FB:FB + 16], rlp_col,
                f32s[:, FC + 32:FC + 48], ALU.mult, ALU.mult))
            comp_done = ("pV", d5)

            gvs = guards.get("VSb")
            vs_ev = None
            for h in range(16):
                w = []
                if h == 0:
                    w = [comp_done]
                    if gvs:
                        w.append(gvs)
                vs_ev = em.op("V", lambda v, h=h, lt=lt:
                              v.tensor_scalar(Dd[:, VS + 64 * h:VS + 64 * (h + 1)],
                                              Cc[:, lt * 1024 + 64 * h:lt * 1024 + 64 * (h + 1)],
                                              f32s[:, FC + 64 + h:FC + 65 + h], None,
                                              ALU.mult),
                              waits=w)
            vs_done = ("pV", vs_ev)
            if DBG_SCAN == "phD":
                return

            for wave in range(2):
                bank = pOUT if wave == 0 else pOUT2
                og = guards.get("OUTw" + str(wave))
                o2 = None
                for hh in range(8):
                    h = wave * 8 + hh
                    w = []
                    if hh == 0:
                        w = [vs_done]
                        if og:
                            w.append(og)
                    em.op("T", lambda t, h=h, hh=hh, bank=bank: t.matmul(
                        bank[:, 64 * hh:64 * hh + 64],
                        stm(h), Dd[:, VS + 64 * h:VS + 64 * (h + 1)],
                        start=(hh == 0), stop=False), waits=w, chain=False)
                    o2 = em.op("T", lambda t, h=h, hh=hh, lt=lt, bank=bank: t.matmul(
                        bank[:, 64 * hh:64 * hh + 64], pairt(h),
                        A[:, AKV + 64 * h:AKV + 64 * (h + 1)],
                        start=False, stop=(hh == 7)), chain=False)
                ev = None
                for hh in range(8):
                    h = wave * 8 + hh
                    ev = em.op("V", lambda v, h=h, hh=hh, lt=lt, bank=bank: v.tensor_scalar(
                        Dd[:, ATK + lt * 1024 + 64 * h:ATK + lt * 1024 + 64 * (h + 1)],
                        bank[:, 64 * hh:64 * hh + 64],
                        f32s[:, FC + 80 + h:FC + 81 + h], None, ALU.mult),
                        waits=([("pT", o2)] if hh == 0 else []))
                guards["OUTw" + str(wave)] = ("pV", ev)
            if DBG_SCAN == "out":
                return

            smm = None
            for h in range(16):
                p, h2 = h // 2, h % 2
                r0 = 64 * h2
                w = []
                if h == 0:
                    g = guards.get("SMrd")
                    if g:
                        w.append(g)
                    g = guards.get("KVrd")
                    if g:
                        w.append(g)
                    w.append(vs_done)
                smm = em.op("T", lambda t, h=h, p=p, r0=r0, h2=h2: t.matmul(
                    pKV[r0:r0 + 64, 64 * p:64 * (p + 1)],
                    Dd[:, KTOK + 128 * p + 64 * h2:KTOK + 128 * p + 64 * h2 + 64],
                    Dd[:, VS + 64 * h:VS + 64 * (h + 1)],
                    start=False, stop=False, skip_group_check=True),
                    waits=w, chain=False)
                smm = em.op("T", lambda t, h=h, p=p, r0=r0, h2=h2: t.matmul(
                    pSM[r0:r0 + 64, 2 * h:2 * h + 2],
                    Dd[:, KTOK + 128 * p + 64 * h2:KTOK + 128 * p + 64 * h2 + 64],
                    Dd[:, KVEC + 2 * h:KVEC + 2 * h + 2],
                    start=False, stop=False, skip_group_check=True),
                    chain=False)
                smm = em.op("T", lambda t, h=h, p=p, r0=r0, h2=h2: t.matmul(
                    pSM[r0:r0 + 64, 32 + 2 * h:32 + 2 * h + 2],
                    Dd[:, QTOK + 128 * p + 64 * h2:QTOK + 128 * p + 64 * h2 + 64],
                    Dd[:, QVEC + 2 * h:QVEC + 2 * h + 2],
                    start=False, stop=False, skip_group_check=True),
                    chain=False)
            st_done = ("pT", smm)
            guards["SMmm"] = st_done
            guards["VSb"] = st_done

            k1a = em.op("V", lambda v: v.tensor_copy(
                A[:, AKV:AKV + 1024].rearrange("p (h c) -> p h c", c=64)[0:64, 0::2, :],
                pKV[0:64, 0:512].rearrange("p (h c) -> p h c", c=64)),
                waits=[st_done])
            k1 = em.op("V", lambda v: v.tensor_copy(
                A[:, AKV:AKV + 1024].rearrange("p (h c) -> p h c", c=64)[64:128, 1::2, :],
                pKV[64:128, 0:512].rearrange("p (h c) -> p h c", c=64)))
            k2 = em.op("V", lambda v: v.tensor_copy(
                Dd[:, KST3:KST3 + 48].rearrange("p (h c) -> p h c", c=3)[:, :, 0::2],
                pSM[:, 0:32].rearrange("p (h c) -> p h c", c=2)))
            k3 = em.op("V", lambda v: v.tensor_copy(
                Dd[:, QST3:QST3 + 48].rearrange("p (h c) -> p h c", c=3)[:, :, 0::2],
                pSM[:, 32:64].rearrange("p (h c) -> p h c", c=2)))
            k4 = em.op("S", lambda s: s.copy(Dd[0:1, SROW:SROW + 80], pSM[0:1, 100:180]),
                       waits=[st_done])
            guards["SMrd"] = ("pS", k4)
            guards["KVrd"] = ("pV", k3)

        def wo_h_phase(half):
            gb_tok = emit_gb_rebuild("g1_row", "b1ln_row", STB,
                                     [("pT", em.last["T"]), ("pV", em.last["V"]),
                                      ("pS", em.last["S"])])
            guards["PACD"] = gb_tok
            guards["SCAN"] = gb_tok
            lt2 = ltT2 = 0
            for i in range(8):
                lt2, ltT2 = emit_transpose_1024(
                    lambda j, i=i: Dd[:, ATK + i * 1024 + 128 * j:ATK + i * 1024 + 128 * (j + 1)],
                    lambda j, i=i: Cc[:, UCM + j * 1024 + i * 128:UCM + j * 1024 + (i + 1) * 128],
                    "ptr", wait_src=([("pV", em.last["V"])] if i == 0 else []),
                    gdict=guards)
            atr = [("pV", lt2), ("pT", ltT2)]
            em.waits("G", atr + [gb_tok])
            bo_d = em.dma("G", Dd[0:1, STM:STM + 1024], D["bo_row"], ch="W")
            wo_d = em.dma("G", Bb[:, 0:8192].rearrange("p (k f) -> p k f", f=1024),
                          D["wo"].rearrange("(k p) f -> p k f", p=128), ch="W")
            for i in range(8):
                ii = half * 8 + i
                for fc in range(2):
                    bi = 4 + fc
                    g = guards.get(bi)
                    w0 = [wo_d, bo_d, gb_tok] + atr
                    if g:
                        w0.append(g)
                    mm = None
                    for k in range(NC):
                        mm = em.op("T", lambda t, k=k, i=i, fc=fc, bi=bi:
                                   t.matmul(pk[bi][:, 0:512],
                                            Cc[:, UCM + k * 1024 + i * 128:UCM + k * 1024 + (i + 1) * 128],
                                            Bb[:, k * 1024 + fc * 512:k * 1024 + (fc + 1) * 512],
                                            start=(k == 0), stop=False),
                                   waits=(w0 if k == 0 else []), chain=False)
                    mm = em.op("T", lambda t, fc=fc, bi=bi:
                               t.matmul(pk[bi][:, 0:512], cs["c_ones_row_bf"][:],
                                        Dd[0:1, STM + fc * 512:STM + (fc + 1) * 512],
                                        start=False, stop=True), chain=False)
                    su = em.op("S", lambda s, ii=ii, fc=fc:
                               s.activation(sc32[:, fc * 512:(fc + 1) * 512],
                                            x_sb[:, ii * 1024 + fc * 512:ii * 1024 + (fc + 1) * 512],
                                            AF.Identity,
                                            bias=f32s[:, 100 + 16 * ii:100 + 16 * ii + 1],
                                            scale=f32s[:, 100 + 16 * ii + 1:100 + 16 * ii + 2]),
                               waits=[("pT", mm)])
                    av = em.op("V", lambda v, fc=fc, bi=bi:
                               v.tensor_add(sc32[:, fc * 512:(fc + 1) * 512],
                                            sc32[:, fc * 512:(fc + 1) * 512],
                                            pk[bi][:, 0:512]),
                               waits=[("pS", su)])
                    guards[bi] = ("pV", av)
                em.op("V", lambda v: v.bn_stats(f32s[:, 2:8], sc32[:, 0:512]))
                em.op("V", lambda v: v.bn_stats(f32s[:, 8:14], sc32[:, 512:1024]))
                t3 = em.op("V", lambda v: v.bn_aggr(f32s[:, 14:16], f32s[:, 2:14]))
                em.op("V", lambda v: v.tensor_scalar(f32s[:, 1:2], f32s[:, 15:16],
                                                     EPS_LN, None, ALU.add))
                t4 = em.op("S", lambda s: s.activation(f32s[:, 1:2], f32s[:, 1:2], AF.Sqrt),
                           waits=[("pV", em.last["V"])])
                em.op("V", lambda v: v.reciprocal(f32s[:, 1:2], f32s[:, 1:2]),
                      waits=[("pS", t4)])
                t6 = em.op("V", lambda v: v.tensor_scalar(f32s[:, 0:1], f32s[:, 14:15],
                                                          f32s[:, 1:2], -1.0,
                                                          ALU.mult, ALU.mult))
                t7 = em.op("S", lambda s, i=i: s.activation(
                    Dd[:, ATK + i * 1024:ATK + (i + 1) * 1024], sc32[:, 0:1024],
                    AF.Identity, bias=f32s[:, 0:1], scale=f32s[:, 1:2]),
                    waits=[("pV", t6)])
                em.op("V", lambda v, i=i: v.tensor_mul(
                    Dd[:, ATK + i * 1024:ATK + (i + 1) * 1024],
                    Dd[:, ATK + i * 1024:ATK + (i + 1) * 1024], g_bc[:]),
                    waits=[("pS", t7)])
                hb = em.op("V", lambda v, i=i: v.tensor_add(
                    Dd[:, ATK + i * 1024:ATK + (i + 1) * 1024],
                    Dd[:, ATK + i * 1024:ATK + (i + 1) * 1024], b_bc[:]))
                lt2, ltT2 = emit_transpose_1024(
                    lambda j, i=i: Dd[:, ATK + i * 1024 + 128 * j:ATK + i * 1024 + 128 * (j + 1)],
                    lambda j, i=i, ii=ii: A[:, j * STRIDE + 1 + ii * 128:j * STRIDE + 1 + (ii + 1) * 128],
                    "ptr", wait_src=[("pV", hb)], gdict=guards)
            return [("pV", lt2), ("pT", ltT2)]

        def ffn_ln2(half, h_done):
            gb_tok = emit_gb_rebuild("g2_row", "b2ln_row", 4096,
                                     [("pT", em.last["T"]), ("pV", em.last["V"]),
                                      ("pS", em.last["S"])] + h_done)
            guards["PACD"] = gb_tok
            guards["SCAN"] = gb_tok
            gelu_ev = None
            for fb in range(32):
                slot = fb % 2
                g = guards.get(("w1", slot))
                if g:
                    em.waits("G", [g])
                dt_ = em.dma("G", Dd[:, slot * 1024:(slot + 1) * 1024]
                             .rearrange("p (k f) -> p k f", f=128),
                             D["w1"].rearrange("(k p) f -> p k f", p=128)[:, :, 128 * fb:128 * (fb + 1)],
                             ch=("A" if slot == 0 else "B"))
                last_mm = None
                for tc in range(2):
                    bi = 4 + (fb * 2 + tc) % 2
                    gb = guards.get(bi)
                    w0 = [dt_] + h_done + [gb_tok]
                    if gb:
                        w0.append(gb)
                    mm = None
                    for k in range(NC):
                        mm = em.op("T", lambda t, k=k, tc=tc, bi=bi, slot=slot, half=half:
                                   t.matmul(pk[bi][:, 0:512],
                                            Dd[:, slot * 1024 + k * 128:slot * 1024 + (k + 1) * 128],
                                            A[:, k * STRIDE + 1 + half * 1024 + tc * 512:
                                              k * STRIDE + 1 + half * 1024 + (tc + 1) * 512],
                                            start=(k == 0), stop=(k == NC - 1)),
                                   waits=(w0 if k == 0 else []), chain=False)
                    buf = Cc if fb < 16 else Bb
                    gelu_ev = em.op("S", lambda s, fb=fb, tc=tc, bi=bi, buf=buf:
                                    s.activation(buf[:, (fb % 16) * 1024 + tc * 512:(fb % 16) * 1024 + (tc + 1) * 512],
                                                 pk[bi][:, 0:512], AF.Gelu,
                                                 bias=b1_c[:, fb:fb + 1], scale=1.0),
                                    waits=[("pT", mm)])
                    guards[bi] = ("pS", gelu_ev)
                    last_mm = mm
                guards[("w1", slot)] = ("pT", last_mm)
            g_done = ("pS", gelu_ev)

            em.waits("G", [("pT", em.last["T"])])
            for db in range(NC):
                slot = db % 2
                g = guards.get(("w2", slot))
                if g:
                    em.waits("G", [g])
                dt_ = em.dma("G", Dd[:, slot * 4096:(slot + 1) * 4096]
                             .rearrange("p (k f) -> p k f", f=128),
                             D["w2"].rearrange("(k p) f -> p k f", p=128)[:, :, 128 * db:128 * (db + 1)],
                             ch=("A" if slot == 0 else "B"))
                last_mm = None
                for tc in range(2):
                    bi = 4 + (db * 2 + tc) % 2
                    gb = guards.get(bi)
                    w0 = [dt_, g_done]
                    if gb:
                        w0.append(gb)
                    mm = None
                    for ft in range(32):
                        buf = Cc if ft < 16 else Bb
                        mm = em.op("T", lambda t, ft=ft, tc=tc, bi=bi, slot=slot, buf=buf:
                                   t.matmul(pk[bi][:, 0:512],
                                            Dd[:, slot * 4096 + ft * 128:slot * 4096 + (ft + 1) * 128],
                                            buf[:, (ft % 16) * 1024 + tc * 512:(ft % 16) * 1024 + (tc + 1) * 512],
                                            start=(ft == 0), stop=(ft == 31)),
                                   waits=(w0 if ft == 0 else []), chain=False)
                    sv = em.op("S", lambda s, db=db, tc=tc, bi=bi:
                               s.activation(Dd[:, 8192 + db * 1024 + tc * 512:8192 + db * 1024 + (tc + 1) * 512],
                                            pk[bi][:, 0:512], AF.Identity,
                                            bias=pcols["b2_col"][:, db:db + 1], scale=1.0),
                               waits=[("pT", mm)])
                    av = em.op("V", lambda v, db=db, tc=tc, half=half:
                               v.tensor_add(A[:, db * STRIDE + 1 + half * 1024 + tc * 512:
                                              db * STRIDE + 1 + half * 1024 + (tc + 1) * 512],
                                            A[:, db * STRIDE + 1 + half * 1024 + tc * 512:
                                              db * STRIDE + 1 + half * 1024 + (tc + 1) * 512],
                                            Dd[:, 8192 + db * 1024 + tc * 512:8192 + db * 1024 + (tc + 1) * 512]),
                               waits=[("pS", sv)])
                    guards[bi] = ("pV", av)
                    last_mm = mm
                guards[("w2", slot)] = ("pT", last_mm)
            s_done = [("pV", em.last["V"])]

            for i in range(8):
                ii = half * 8 + i
                lt2, ltT2 = emit_transpose_1024(
                    lambda j, i=i, half=half: A[:, j * STRIDE + 1 + half * 1024 + i * 128:
                                                j * STRIDE + 1 + half * 1024 + (i + 1) * 128],
                    lambda j, i=i: Dd[:, 8192 + i * 1024 + 128 * j:8192 + i * 1024 + 128 * (j + 1)],
                    "ptr", wait_src=(s_done if i == 0 else []), gdict=guards)
                stok = Dd[:, 8192 + i * 1024:8192 + (i + 1) * 1024]
                em.op("V", lambda v, stok=stok: v.bn_stats(f32s[:, 2:8], stok[:, 0:512]),
                      waits=[("pV", lt2)])
                em.op("V", lambda v, stok=stok: v.bn_stats(f32s[:, 8:14], stok[:, 512:1024]))
                em.op("V", lambda v: v.bn_aggr(f32s[:, 14:16], f32s[:, 2:14]))
                em.op("V", lambda v: v.tensor_scalar(f32s[:, 1:2], f32s[:, 15:16],
                                                     EPS_LN, None, ALU.add))
                t4 = em.op("S", lambda s: s.activation(f32s[:, 1:2], f32s[:, 1:2], AF.Sqrt),
                           waits=[("pV", em.last["V"])])
                em.op("V", lambda v: v.reciprocal(f32s[:, 1:2], f32s[:, 1:2]),
                      waits=[("pS", t4)])
                t6 = em.op("V", lambda v: v.tensor_scalar(f32s[:, 0:1], f32s[:, 14:15],
                                                          f32s[:, 1:2], -1.0,
                                                          ALU.mult, ALU.mult))
                t7 = em.op("S", lambda s, stok=stok: s.activation(
                    stok, stok, AF.Identity, bias=f32s[:, 0:1], scale=f32s[:, 1:2]),
                    waits=[("pV", t6)])
                em.op("V", lambda v, stok=stok: v.tensor_mul(sc32[:, 0:1024], stok, g_bc[:]),
                      waits=[("pS", t7)])
                em.op("V", lambda v: v.tensor_add(sc32[:, 0:1024], sc32[:, 0:1024], b_bc[:]))
                em.op("V", lambda v, ii=ii: v.tensor_add(
                    x_sb[:, ii * 1024:(ii + 1) * 1024],
                    x_sb[:, ii * 1024:(ii + 1) * 1024], sc32[:, 0:1024]))

        h_done = None
        for half in range(2):
            u_done = emit_ln_half(half)
            if half == 1:
                u_done = u_done + [("pT", em.last["T"]), ("pV", em.last["V"])]
            if DBG_STOP == "ln":
                em.barrier(); return
            qd = proj_qk("q", 0, pcols["bq_col"], u_done)
            kd = proj_qk("k", 8192, pcols["bk_col"], u_done)
            vd = proj_v(u_done)
            qkv_done = [qd, kd, vd]
            if DBG_STOP == "proj":
                em.barrier(); return
            for lt in range(8):
                scan_chunk(half * 8 + lt, qkv_done if lt == 0 else [])
                if DBG_STOP == "chunk0" and lt == 0:
                    em.barrier(); return
            if DBG_STOP == "scan":
                em.barrier(); return
            h_done = wo_h_phase(half)
            if DBG_STOP == "wo":
                em.barrier(); return
        for half in range(2):
            ffn_ln2(half, h_done)
        em.barrier()

    for k in CONST_SPECS:
        em.dma("Y", cs[k][:], D[k])
    if not skip_params:
        for nm in PARAM_COLS:
            em.dma("Y", pcols[nm][:], D[nm])
        em.dma("Y", b1_c[:], D["b1_col"])
    em.barrier()

    for b in range(B_PER_CORE):
        src_b = x_in.rearrange("b (i p) c -> b i p c", p=128)[b]
        xload_toks.clear()
        for i in range(NT):
            xload_toks[i] = em.dma("S", x_sb[:, i * 1024:(i + 1) * 1024], src_b[i])
        subs_b = subs[b] if isinstance(subs, dict) else subs
        if not subs_b or subs_b[0] != 1:
            em.barrier()
            xload_toks.clear()
        for s_id in subs_b:
            if s_id == 1:
                emit_attn_a()
            if s_id == 3:
                emit_attn_b()
            if s_id in (2, 4):
                emit_ffn()
        em.dma("Y", out.rearrange("b (i p) c -> b p i c", p=128)[b],
               x_sb[:].rearrange("p (i c) -> p i c", c=1024))
        em.barrier()

    em.waits("Y", em.snap())

    with nc.Block() as blk:
        em.build(blk)
    rem = nc.sbuf_bytes_remaining
    st.close()
    return nc, rem


def make_inputs_per_core(inputs, core):
    x_full = np.asarray(inputs["L0_attn_a_query"], np.float32)
    W = {
        "wipT": np.ascontiguousarray(np.asarray(inputs["L0_attn_a_in_proj_weight"], np.float32).T),
        "woaT": np.ascontiguousarray(np.asarray(inputs["L0_attn_a_out_proj_weight"], np.float32).T),
        "wrT": np.ascontiguousarray(np.asarray(inputs["L0_ffn_W_r"], np.float32).T),
        "wkT": np.ascontiguousarray(np.asarray(inputs["L0_ffn_W_k"], np.float32).T),
        "wvT": np.ascontiguousarray(np.asarray(inputs["L0_ffn_W_v"], np.float32).T),
        "wq": np.ascontiguousarray(np.asarray(inputs["L0_attn_b_Wq"], np.float32)),
        "wk": np.ascontiguousarray(np.asarray(inputs["L0_attn_b_Wk"], np.float32)),
        "wv": np.ascontiguousarray(np.asarray(inputs["L0_attn_b_Wv"], np.float32)),
        "wo": np.ascontiguousarray(np.asarray(inputs["L0_attn_b_Wo"], np.float32)),
        "w1": np.ascontiguousarray(np.asarray(inputs["L0_attn_b_W1"], np.float32)),
        "w2": np.ascontiguousarray(np.asarray(inputs["L0_attn_b_W2"], np.float32)),
    }
    for nm, key in [("mur_col", "L0_ffn_mu_r"), ("muk_col", "L0_ffn_mu_k"),
                    ("bq_col", "L0_attn_b_bq"), ("bk_col", "L0_attn_b_bk"),
                    ("bv_col", "L0_attn_b_bv"), ("bo_col", "L0_attn_b_bo"),
                    ("b2_col", "L0_attn_b_b2")]:
        W[nm] = np.ascontiguousarray(np.asarray(inputs[key], np.float32).reshape(8, 128).T)
    W["b1_col"] = np.ascontiguousarray(
        np.asarray(inputs["L0_attn_b_b1"], np.float32).reshape(32, 128).T)
    W["bv_row"] = np.asarray(inputs["L0_attn_b_bv"], np.float32).reshape(1, C)
    W["bo_row"] = np.asarray(inputs["L0_attn_b_bo"], np.float32).reshape(1, C)
    W["g1_row"] = np.asarray(inputs["L0_attn_b_ln1_g"], np.float32).reshape(1, C)
    W["b1ln_row"] = np.asarray(inputs["L0_attn_b_ln1_b"], np.float32).reshape(1, C)
    W["g2_row"] = np.asarray(inputs["L0_attn_b_ln2_g"], np.float32).reshape(1, C)
    W["b2ln_row"] = np.asarray(inputs["L0_attn_b_ln2_b"], np.float32).reshape(1, C)
    m = {"x": np.ascontiguousarray(x_full[2 * core:2 * core + 2])}
    m.update(W)
    m.update(_consts_host())
    return m


def kernel(**inputs):
    subs = inputs.pop("_subs", (1, 2, 3, 4))
    nc, rem = build_program(subs=subs)
    in_maps = [make_inputs_per_core(inputs, core) for core in range(8)]
    res = run_bass_kernel_spmd(nc, in_maps, core_ids=list(range(8)))
    outs = [res.results[i]["out"] for i in range(8)]
    return np.concatenate(outs, axis=0).astype(np.float32)
